# version 08
# baseline (speedup 1.0000x reference)
import math
import numpy as np
import ml_dtypes
import concourse.bass as bass
import concourse.mybir as mybir
from concourse.bass_utils import run_bass_kernel_spmd
from contextlib import ExitStack

F32 = mybir.dt.float32
BF16 = mybir.dt.bfloat16
ALU = mybir.AluOpType
AF = mybir.ActivationFunctionType
AX = mybir.AxisListType.X

B, T, C, H, D, DFF = 2, 4096, 2048, 16, 128, 8192
NCORES = 8
NT = 8
TOK = NT * 128
MAGIC = float(np.float32(1.5 * 2 ** 23))
SCALE_BASE = 512.0
NELEM_W = float(DFF * C)

DO_FFN1 = True
DBG = False
DO_ATTN = True
DO_FFN2 = True

PASS_LIST = [
    (0, ("diag",)), (0, ("slot", 0, 0)), (0, ("slot", 1, 0)), (0, ("slot", 2, 0)),
    (1, ("diag",)), (1, ("slot", 0, 0)), (1, ("slot", 1, 0)), (1, ("slot", 2, 0)),
    (1, ("slot", 3, 0)), (1, ("slot", 3, 1)), (1, ("slot", 2, 1)), (1, ("slot", 1, 1)),
]
NPASS = len(PASS_LIST)


def _gammas():
    return 1.0 - 2.0 ** (-5.0 - np.arange(H, dtype=np.float64))


def _chunks_of(core):
    r = core % 4
    return r, 7 - r


def _xpos_tables(positions):
    dim_scale = (np.arange(0, C, 2, dtype=np.float64) + 0.4 * C) / (1.4 * C)
    pos = positions.astype(np.float64)
    scale = dim_scale[None, :] ** (pos[:, None] / SCALE_BASE)
    inv_freq = 1.0 / 10000.0 ** (np.arange(0, C, 2, dtype=np.float64) / C)
    sinu = pos[:, None] * inv_freq[None, :]
    sin = np.repeat(np.sin(sinu) * scale, 2, axis=-1)
    cos = np.repeat(np.cos(sinu) * scale, 2, axis=-1)
    bs = sin.copy()
    bs[:, 0::2] = -sin[:, 0::2]
    return cos, bs


def _sctab_for_core(core):
    cA, cB = _chunks_of(core)
    gam = _gammas()
    tab = np.zeros((H, NPASS, 4), dtype=np.float64)
    for p, (qc, src) in enumerate(PASS_LIST):
        c_own = cA if qc == 0 else cB
        tc0 = c_own * 512
        if src[0] == "diag":
            s_chunk, valid = c_own, True
        else:
            slot, sub = src[1], src[2]
            s_chunk = slot if sub == 0 else 7 - slot
            valid = s_chunk < c_own
        if not valid:
            continue
        for q in range(4):
            s0 = s_chunk * 512 + q * 128
            for h in range(H):
                tab[h, p, q] = gam[h] ** (tc0 - s0)
    with np.errstate(under="ignore"):
        flat = tab.reshape(1, -1).astype(np.float32)
    return np.ascontiguousarray(np.broadcast_to(flat, (128, flat.shape[1])))


def _shared_inputs(w_qkv, w_g, w_proj, gnw, gnb):
    gam = _gammas()
    p_ = np.arange(128)
    qdc = np.zeros((128, H * 4), np.float64)
    for h in range(H):
        for j in range(4):
            qdc[:, h * 4 + j] = gam[h] ** (128 * j + p_) / math.sqrt(D)
    kdc = np.zeros((128, H), np.float64)
    for h in range(H):
        kdc[:, h] = gam[h] ** (-p_.astype(np.float64))
    tri = np.zeros((128, 4 * 512), np.float32)
    f_ = np.arange(512)
    for q in range(4):
        tri[:, q * 512:(q + 1) * 512] = (
            f_[None, :] >= (p_[:, None] + 128 * q)).astype(np.float32)
    return dict(
        wqkv=np.ascontiguousarray(w_qkv, np.float32),
        wg=np.ascontiguousarray(w_g, np.float32),
        wproj=np.ascontiguousarray(w_proj, np.float32),
        gnw_r=np.ascontiguousarray(gnw.reshape(H, 128).T, np.float32),
        gnb_r=np.ascontiguousarray(gnb.reshape(H, 128).T, np.float32),
        qdec=qdc.astype(np.float32),
        kdec=kdc.astype(np.float32),
        tri=tri.astype(ml_dtypes.bfloat16),
        ident=np.eye(128, dtype=ml_dtypes.bfloat16),
        identf=np.eye(128, dtype=np.float32),
        ones_r=np.ones((1, 128), np.float32),
    )


def _host_inputs(core, x, w1, w2, w3, shared):
    b = core // 4
    cA, cB = _chunks_of(core)
    xs = np.concatenate(
        [x[b, cA * 512:(cA + 1) * 512], x[b, cB * 512:(cB + 1) * 512]], axis=0)
    pos = np.concatenate(
        [np.arange(cA * 512, (cA + 1) * 512), np.arange(cB * 512, (cB + 1) * 512)])
    cos, bs = _xpos_tables(pos)
    d = dict(
        x_in=np.ascontiguousarray(xs, dtype=np.float32),
        w1s=np.ascontiguousarray(w1.T[core * 256:(core + 1) * 256]),
        w2s=np.ascontiguousarray(w2.T[core * 256:(core + 1) * 256]),
        w3s=np.ascontiguousarray(w3.T[core * 1024:(core + 1) * 1024]),
        cosb=cos.astype(np.float32),
        sinb=bs.astype(np.float32),
        sctab=_sctab_for_core(core),
    )
    d.update(shared)
    return d


class Sched:
    def __init__(self):
        self.steps = []
        self.cnt = {}

    def n(self, sem):
        return self.cnt.get(id(sem), 0)

    def add(self, eng, fn, waits=(), inc=None, incv=1):
        if inc is not None:
            self.cnt[id(inc)] = self.n(inc) + incv
        seen = {}
        for w in waits:
            if w is None:
                continue
            s, v = w
            if id(s) in seen:
                seen[id(s)] = (s, max(seen[id(s)][1], v))
            else:
                seen[id(s)] = (s, v)
        self.steps.append((eng, fn, list(seen.values()),
                           [(inc, incv)] if inc is not None else []))

    def replay(self, name, eng):
        for en, fn, waits, incs in self.steps:
            if en != name:
                continue
            for sem, v in waits:
                eng.wait_ge(sem, v)
            instr = fn(eng)
            for sem, v in incs:
                instr.then_inc(sem, v)


def build_nc():
    nc = bass.Bass("TRN2", target_bir_lowering=False)
    ap = lambda t: t.ap()

    x_in = ap(nc.dram_tensor("x_in", [TOK, C], F32, kind="ExternalInput"))
    wqkv = ap(nc.dram_tensor("wqkv", [C, 3 * C], F32, kind="ExternalInput"))
    wgd = ap(nc.dram_tensor("wg", [C, C], F32, kind="ExternalInput"))
    wpd = ap(nc.dram_tensor("wproj", [C, C], F32, kind="ExternalInput"))
    gnw_in = ap(nc.dram_tensor("gnw_r", [128, H], F32, kind="ExternalInput"))
    gnb_in = ap(nc.dram_tensor("gnb_r", [128, H], F32, kind="ExternalInput"))
    w1s = ap(nc.dram_tensor("w1s", [256, DFF], F32, kind="ExternalInput"))
    w2s = ap(nc.dram_tensor("w2s", [256, DFF], F32, kind="ExternalInput"))
    w3s = ap(nc.dram_tensor("w3s", [1024, C], F32, kind="ExternalInput"))
    cosb = ap(nc.dram_tensor("cosb", [TOK, C], F32, kind="ExternalInput"))
    sinb = ap(nc.dram_tensor("sinb", [TOK, C], F32, kind="ExternalInput"))
    qdec = ap(nc.dram_tensor("qdec", [128, H * 4], F32, kind="ExternalInput"))
    kdec = ap(nc.dram_tensor("kdec", [128, H], F32, kind="ExternalInput"))
    tri_in = ap(nc.dram_tensor("tri", [128, 4 * 512], BF16, kind="ExternalInput"))
    sctab_in = ap(nc.dram_tensor("sctab", [128, H * NPASS * 4], F32,
                                 kind="ExternalInput"))
    ident_in = ap(nc.dram_tensor("ident", [128, 128], BF16, kind="ExternalInput"))
    identf_in = ap(nc.dram_tensor("identf", [128, 128], F32, kind="ExternalInput"))
    ones_in = ap(nc.dram_tensor("ones_r", [1, 128], F32, kind="ExternalInput"))

    out_c = ap(nc.dram_tensor("out_c", [TOK, C], F32, kind="ExternalOutput"))
    if DBG:
        dbg_st = ap(nc.dram_tensor("dbg_st", [128, 384], F32, kind="ExternalOutput"))
        dbg_row = ap(nc.dram_tensor("dbg_row", [1, 144], F32, kind="ExternalOutput"))
        dbg_bc = ap(nc.dram_tensor("dbg_bc", [128, 8], F32, kind="ExternalOutput"))
        dbg_afm = ap(nc.dram_tensor("dbg_afm", [128, 16384], BF16, kind="ExternalOutput"))
        dbg_h = ap(nc.dram_tensor("dbg_h", [128, DFF], F32, kind="ExternalOutput"))

    x_dram = ap(nc.dram_tensor("x_dram", [TOK, C], F32, kind="Internal"))
    h_dram = ap(nc.dram_tensor("h_dram", [TOK, DFF], F32, kind="Internal"))
    ag1_in = ap(nc.dram_tensor("ag1_in", [512, DFF], BF16, kind="Internal"))
    ag1_out = ap(nc.dram_tensor("ag1_out", [NCORES * 512, DFF], BF16, kind="Internal"))
    ag2_in = ap(nc.dram_tensor("ag2_in", [1024, C], BF16, kind="Internal"))
    ag2_out = ap(nc.dram_tensor("ag2_out", [NCORES * 1024, C], BF16, kind="Internal"))
    ar_in = ap(nc.dram_tensor("ar_in", [1, 4], F32, kind="Internal"))
    ar_out = ap(nc.dram_tensor("ar_out", [1, 4], F32, kind="Internal"))
    pflip = ap(nc.dram_tensor("pflip", [128, 1], F32, kind="Internal"))
    k_in = ap(nc.dram_tensor("k_in", [H * 128, TOK], BF16, kind="Internal"))
    k_out = ap(nc.dram_tensor("k_out", [4 * H * 128, TOK], BF16, kind="Internal"))
    v_in = ap(nc.dram_tensor("v_in", [H * TOK, D], BF16, kind="Internal"))
    v_out = ap(nc.dram_tensor("v_out", [4 * H * TOK, D], BF16, kind="Internal"))

    def w1t_rows(k):
        return (k // 2) * 512 + (k % 2) * 128

    def w2t_rows(k):
        return (k // 2) * 512 + (k % 2) * 128 + 256

    def w3t_rows(k):
        return (k // 8) * 1024 + (k % 8) * 128

    sch = Sched()

    with ExitStack() as ctx:
        e = ctx.enter_context
        xt = [e(nc.sbuf_tensor(f"xt{i}", [128, C], F32)) for i in range(2)]
        fm = e(nc.sbuf_tensor("fm", [128, 32768], BF16))
        wbuf = e(nc.sbuf_tensor("wbuf", [128, 8192], BF16))
        big3 = e(nc.sbuf_tensor("big3", [128, 4096], F32))
        aux = e(nc.sbuf_tensor("aux", [128, 2048], BF16))
        kst = e(nc.sbuf_tensor("kst", [128, 3 * 512], BF16))
        vst = e(nc.sbuf_tensor("vst", [128, 3 * 512], BF16))
        csb = e(nc.sbuf_tensor("csb", [128, 1536], F32))
        w3b = e(nc.sbuf_tensor("w3b", [128, 6144], BF16))
        hstg = e(nc.sbuf_tensor("hstg", [128, 2048], F32))
        smb = e(nc.sbuf_tensor("smb", [128, 4 * 512], BF16))
        qst = e(nc.sbuf_tensor("qst", [128, 2048], F32))
        gch = e(nc.sbuf_tensor("gch", [128, DFF], BF16))
        sqs = e(nc.sbuf_tensor("sqs", [128, C], F32))
        qraw = e(nc.sbuf_tensor("qraw", [128, 2048], F32))
        tri = e(nc.sbuf_tensor("tri_s", [128, 4 * 512], BF16))
        gnw = e(nc.sbuf_tensor("gnw_s", [128, H], F32))
        gnb = e(nc.sbuf_tensor("gnb_s", [128, H], F32))
        sct = e(nc.sbuf_tensor("sct_s", [128, H * NPASS * 4], F32))
        qde = e(nc.sbuf_tensor("qde_s", [128, H * 4], F32))
        kde = e(nc.sbuf_tensor("kde_s", [128, H], F32))
        idn = e(nc.sbuf_tensor("idn_s", [128, 128], BF16))
        idnf = e(nc.sbuf_tensor("idnf_s", [128, 128], F32))
        ones1 = e(nc.sbuf_tensor("ones1", [1, 128], F32))
        st = e(nc.sbuf_tensor("stats", [128, 384], F32))
        sv = e(nc.sbuf_tensor("sv", [128, 16], F32))
        srow = e(nc.sbuf_tensor("srow", [1, 144], F32))
        bc = e(nc.sbuf_tensor("bc", [128, 8], F32))

        pb = [e(nc.psum_tensor(f"pb{i}", [128, 512], F32)) for i in range(6)]
        ptr2 = [e(nc.psum_tensor(f"ptr{i}", [128, 1024], BF16)) for i in range(2)]

        sem_names = ["cst", "xcp", "bl", "p0f", "agw", "wl0", "wl1",
                     "xl0", "xl1", "hs0", "hs1", "hrl", "w3m0", "w3m1",
                     "w3t0", "w3t1", "csm0", "csm1", "csm2", "csm3", "hs2", "hs3",
                     "hs4", "hs5", "hs6", "hs7", "xl2", "xl3",
                     "w3a0", "w3a1", "w3a2", "w3a3", "w3a4", "w3a5",
                     "qw0", "qw1",
                     "klm0", "klm1", "klm2", "vlm0", "vlm1", "vlm2",
                     "kvw", "xw1", "xw2", "xw3", "gwl0", "gwl1", "gdbg"]
        SM = {nm: e(nc.semaphore(nm)) for nm in sem_names}
        vS = e(nc.semaphore("vS"))
        sS = e(nc.semaphore("sS"))
        tS = e(nc.semaphore("tS"))
        gS = e(nc.semaphore("gS"))
        ccS = e(nc.semaphore("ccS"))

        block = e(nc.Block())

        ESEM = {"vector": vS, "scalar": sS, "gpsimd": gS, "tensor": tS}

        def dma(queue, sem, out, in_, waits=()):
            s = SM[sem]
            sch.add(queue, lambda E, o=out, i=in_: E.dma_start(out=o, in_=i),
                    waits=waits, inc=s, incv=16)
            return (s, sch.n(s))

        def comp(eng, fn, waits=()):
            s = ESEM[eng]
            sch.add(eng, fn, waits=waits, inc=s)
            return (s, sch.n(s))

        def coll(fn, waits=()):
            sch.add("gpsimd", fn, waits=waits, inc=ccS, incv=1)
            return (ccS, sch.n(ccS))

        bank_fill = [(tS, 0)] * 6
        bank_free = [None] * 6
        ptr_fill = [(tS, 0)] * 2
        ptr_free = [None, None]
        buf_rd = {}

        def rd(buf):
            return buf_rd.get(id(buf))

        def setrd(buf, dep):
            old = buf_rd.get(id(buf))
            if old is not None and old[0] is dep[0]:
                dep = (dep[0], max(dep[1], old[1]))
            buf_rd[id(buf)] = dep

        for dst, src in ((tri, tri_in), (gnw, gnw_in), (gnb, gnb_in),
                         (sct, sctab_in), (qde, qdec), (kde, kdec),
                         (idn, ident_in), (idnf, identf_in),
                         (ones1, ones_in)):
            dma("sync", "cst", dst[:], src)
        dep_consts = (SM["cst"], sch.n(SM["cst"]))
        dma("gpsimd", "xcp", x_dram, x_in)
        dep_xcopy = (SM["xcp"], sch.n(SM["xcp"]))

        def p0_sum(slc, rows, cols, col):
            dvl = None
            pieces = [(r, c0) for r in range(rows // 128)
                      for c0 in range(0, cols, 4096)]
            for pi, (r, c0) in enumerate(pieces):
                wdt = min(4096, cols - c0)
                dd = dma("sync", "bl", big3[:, 0:wdt],
                         slc[r * 128:(r + 1) * 128, c0:c0 + wdt],
                         waits=[dvl, rd(big3)])
                tgt = st[:, col:col + 1] if pi == 0 else st[:, col + 4:col + 5]
                dvl = comp("vector",
                           lambda E, tg=tgt, wd=wdt: E.reduce_sum(
                               tg, big3[:, 0:wd], AX, apply_absolute_value=True),
                           waits=[dd])
                setrd(big3, dvl)
                if pi > 0:
                    dvl = comp("vector", lambda E, c=col: E.tensor_add(
                        st[:, c:c + 1], st[:, c:c + 1], st[:, c + 4:c + 5]),
                        waits=[dvl])
            return dvl

        d1 = p0_sum(w1s, 256, DFF, 0)
        d2 = p0_sum(w2s, 256, DFF, 1)
        d3 = p0_sum(w3s, 1024, C, 2)
        rdep = None
        for j, dep in enumerate((d1, d2, d3)):
            f1 = dma("scalar", "p0f", pflip, st[:, j:j + 1], waits=[dep, rdep])
            f2 = dma("scalar", "p0f", srow[0:1, 0:128],
                     pflip.rearrange("p one -> one p"), waits=[f1])
            rdep = comp("vector", lambda E, j=j: E.reduce_sum(
                srow[0:1, 128 + j:129 + j], srow[0:1, 0:128], AX), waits=[f2])
        dz = comp("vector", lambda E: E.memset(srow[0:1, 131:132], 0.0))
        da = dma("scalar", "p0f", ar_in, srow[0:1, 128:132], waits=[rdep, dz])
        dar = coll(lambda E: E.collective_compute(
            "AllReduce", ALU.add, replica_groups=[list(range(NCORES))],
            ins=[ar_in.opt()], outs=[ar_out.opt()]), waits=[da])
        dld = dma("scalar", "p0f", srow[0:1, 136:140], ar_out, waits=[dar])
        dmw = comp("vector", lambda E: E.tensor_scalar(
            srow[0:1, 132:135], srow[0:1, 136:139], 1.0 / NELEM_W / 127.0, None,
            ALU.mult), waits=[dld])
        dmm_ = comp("vector", lambda E: E.tensor_scalar(
            srow[0:1, 136:139], srow[0:1, 136:139], 1.0 / NELEM_W, None,
            ALU.mult), waits=[dmw])
        div = comp("vector", lambda E: E.reciprocal(
            srow[0:1, 140:143], srow[0:1, 136:139]), waits=[dmm_])
        dg1 = comp("vector", lambda E: E.tensor_copy(
            srow[0:1, 0:3], srow[0:1, 132:135]), waits=[div])
        dg2 = comp("vector", lambda E: E.tensor_copy(
            srow[0:1, 3:6], srow[0:1, 140:143]), waits=[dg1])
        dbc = comp("tensor", lambda E: E.matmul(
            pb[5][:, 0:6], ones1[0:1, :], srow[0:1, 0:6], start=True, stop=True),
            waits=[dg2, dep_consts, bank_free[5]])
        bank_fill[5] = dbc
        dbc2 = comp("vector", lambda E: E.tensor_copy(bc[:, 0:6], pb[5][:, 0:6]),
                    waits=[dbc])
        bank_free[5] = dbc2

        def p0_tern(slc, rows, cols, ivcol, dst):
            dprev = None
            pieces = [(r, c0) for r in range(rows // 128)
                      for c0 in range(0, cols, 4096)]
            for r, c0 in pieces:
                wdt = min(4096, cols - c0)
                dd = dma("sync", "bl", big3[:, 0:wdt],
                         slc[r * 128:(r + 1) * 128, c0:c0 + wdt],
                         waits=[dbc2, dprev, rd(big3)])
                dv = comp("vector", lambda E, wd=wdt, c=ivcol: E.tensor_scalar(
                    big3[:, 0:wd], big3[:, 0:wd], bc[:, c:c + 1], MAGIC,
                    ALU.mult, ALU.add), waits=[dd])
                dv = comp("vector", lambda E, wd=wdt: E.tensor_scalar(
                    big3[:, 0:wd], big3[:, 0:wd], MAGIC, 1.0,
                    ALU.subtract, ALU.min), waits=[dv])
                dv = comp("vector", lambda E, wd=wdt: E.tensor_scalar(
                    wbuf[:, 0:wd], big3[:, 0:wd], -1.0, None, ALU.max),
                    waits=[dv, rd(wbuf)])
                setrd(big3, dv)
                dprev = dma("scalar", "agw", dst[r * 128:(r + 1) * 128, c0:c0 + wdt],
                            wbuf[:, 0:wdt], waits=[dv])
                setrd(wbuf, dprev)
            return dprev

        dt1 = p0_tern(w1s, 256, DFF, 3, ag1_in[0:256, :])
        dt2 = p0_tern(w2s, 256, DFF, 4, ag1_in[256:512, :])
        dt3 = p0_tern(w3s, 1024, C, 5, ag2_in)
        dag1 = coll(lambda E: E.collective_compute(
            "AllGather", ALU.bypass, replica_groups=[list(range(NCORES))],
            ins=[ag1_in.opt()], outs=[ag1_out.opt()]), waits=[dt1, dt2])
        dag2 = coll(lambda E: E.collective_compute(
            "AllGather", ALU.bypass, replica_groups=[list(range(NCORES))],
            ins=[ag2_in.opt()], outs=[ag2_out.opt()]), waits=[dt3])

        xt_free = [None, None]
        phase_entry = [dep_xcopy]

        def load_xtile(t, src=x_dram):
            slot = t % 2
            d = dma("sync", f"xl{slot}", xt[slot][:], src[t * 128:(t + 1) * 128, :],
                    waits=[xt_free[slot], phase_entry[0]])
            return d, slot

        def ln_stats(t, slot, dxt, want_minmax):
            b = t * 12
            dsum = comp("vector", lambda E, b=b, slot=slot: E.reduce_sum(
                st[:, b:b + 1], xt[slot][:], AX), waits=[dxt])
            dsq = comp("scalar", lambda E, b=b, slot=slot: E.activation(
                sqs[:], xt[slot][:], AF.Square, accum_out=st[:, b + 1:b + 2]),
                waits=[dxt, rd(sqs)])
            setrd(sqs, dsq)
            deps = [dsum, dsq]
            if want_minmax:
                deps.append(comp("vector", lambda E, b=b, slot=slot:
                                 E.tensor_reduce(
                    st[:, b + 2:b + 3], xt[slot][:], AX, ALU.max), waits=[dxt]))
                deps.append(comp("vector", lambda E, b=b, slot=slot:
                                 E.tensor_reduce(
                    st[:, b + 3:b + 4], xt[slot][:], AX, ALU.min), waits=[dxt]))
            return deps

        def transpose16(src_bf, dsrc, dst_of, nt_=16, extra_waits=()):
            dv = None
            for h0 in range(0, nt_, 8):
                n = min(8, nt_ - h0)
                half = (h0 // 8) % 2
                pt = ptr2[half]
                dt = None
                for j in range(n):
                    dt = comp("tensor", lambda E, j=j, jj=h0 + j, pt=pt:
                              E.transpose(
                        pt[:, j * 128:(j + 1) * 128],
                        src_bf[:, jj * 128:(jj + 1) * 128], idn[:]),
                        waits=[dsrc, ptr_free[half], dep_consts] if j == 0
                        else [dsrc])
                    ptr_fill[half] = dt
                dv = comp("vector", lambda E, h0=h0, n=n, pt=pt:
                          E.tensor_copy(
                    dst_of(h0, n),
                    pt[:, 0:n * 128]
                    .rearrange("p (nn z) -> p nn z", z=128)),
                    waits=[dt] + list(extra_waits))
                ptr_free[half] = dv
            return dv

        def fm_dst(buf, stride, base, toff):
            def f(j0, n, buf=buf, stride=stride, base=base, toff=toff):
                v = buf[:].rearrange("p (ct z) -> p ct z", z=stride)
                return v[:, base + j0: base + j0 + n, toff:toff + 128]
            return f

        def ffn_phase(dst, wsem):
            dcoefs = [None] * NT
            for t in range(NT):
                dxt, slot = load_xtile(t)
                deps = ln_stats(t, slot, dxt, True)
                b = t * 12
                cS = lambda i, b=b: st[:, b + i:b + i + 1]
                d = comp("vector", lambda E, cS=cS: E.tensor_scalar(
                    cS(9), cS(0), 1.0 / C, None, ALU.mult), waits=deps)
                d = comp("vector", lambda E, cS=cS: E.tensor_scalar(
                    cS(10), cS(1), 1.0 / C, None, ALU.mult), waits=[d])
                d = comp("vector", lambda E, cS=cS: E.scalar_tensor_tensor(
                    cS(11), cS(9), 1.0, cS(9), ALU.mult, ALU.mult), waits=[d])
                d = comp("vector", lambda E, cS=cS: E.tensor_sub(
                    cS(10), cS(10), cS(11)), waits=[d])
                d = comp("vector", lambda E, cS=cS: E.tensor_scalar(
                    cS(11), cS(10), 1.000001, 1e-11, ALU.mult, ALU.add), waits=[d])
                d = comp("scalar", lambda E, cS=cS: E.activation(
                    cS(11), cS(11), AF.Sqrt), waits=[d])
                d = comp("vector", lambda E, cS=cS: E.reciprocal(cS(11), cS(11)),
                         waits=[d])
                d2 = comp("vector", lambda E, cS=cS: E.tensor_sub(
                    cS(2), cS(2), cS(9)), waits=[d])
                d2 = comp("vector", lambda E, cS=cS: E.tensor_sub(
                    cS(3), cS(9), cS(3)), waits=[d2])
                d2 = comp("vector", lambda E, cS=cS: E.tensor_tensor(
                    cS(10), cS(2), cS(3), ALU.max), waits=[d2])
                d2 = comp("vector", lambda E, cS=cS: E.tensor_mul(
                    cS(8), cS(10), cS(11)), waits=[d2])
                d2 = comp("vector", lambda E, cS=cS: E.tensor_scalar(
                    cS(8), cS(8), 1e-5, None, ALU.max), waits=[d2])
                d2 = comp("vector", lambda E, cS=cS: E.reciprocal(
                    cS(2), cS(8)), waits=[d2])
                d2 = comp("vector", lambda E, cS=cS: E.tensor_mul(
                    cS(4), cS(11), cS(2)), waits=[d2])
                d2 = comp("vector", lambda E, cS=cS: E.tensor_scalar(
                    cS(4), cS(4), 127.0, None, ALU.mult), waits=[d2])
                d2 = comp("vector", lambda E, cS=cS: E.scalar_tensor_tensor(
                    cS(5), cS(9), -1.0, cS(4), ALU.mult, ALU.mult), waits=[d2])
                d2 = comp("vector", lambda E, cS=cS: E.tensor_mul(
                    cS(6), cS(8), bc[:, 0:1]), waits=[d2, dbc2])
                d2 = comp("vector", lambda E, cS=cS: E.tensor_mul(
                    cS(7), cS(8), bc[:, 1:2]), waits=[d2])
                du = comp("scalar", lambda E, slot=slot, cS=cS: E.activation(
                    sqs[:], xt[slot][:], AF.Identity, bias=cS(5), scale=cS(4)),
                    waits=[d2, rd(sqs)])
                du = comp("vector", lambda E: E.tensor_scalar(
                    sqs[:], sqs[:], MAGIC, MAGIC, ALU.add, ALU.subtract),
                    waits=[du])
                du = comp("vector", lambda E: E.tensor_scalar(
                    aux[:], sqs[:], 127.0, -128.0, ALU.min, ALU.max),
                    waits=[du, rd(aux)])
                setrd(sqs, du)
                dq = transpose16(aux, du, fm_dst(fm, 1024, 0, t * 128),
                                 extra_waits=[rd(fm)])
                setrd(aux, dq)
                xt_free[slot] = du
                dm0 = comp("vector", lambda E, cS=cS: E.memset(cS(2), 0.0),
                           waits=[d2])
                dm1 = comp("vector", lambda E, cS=cS: E.memset(cS(3), 0.0),
                           waits=[d2])
                dcoefs[t] = (d2, dm0, dm1)
                afm_all = dq

            wslot_free = [None, None]
            hsa_free = [None] * NT
            hslot_free = hslot_free_attn
            last_mm12 = None
            for o in range(DFF // 256):
                for w12 in range(2):
                    ci = 2 * o + w12
                    sl = ci % 2
                    rows = w1t_rows if w12 == 0 else w2t_rows
                    dwl = None
                    for kk in range(0, 16, 2):
                        dwl = dma(
                            "sync", f"wl{sl}",
                            wbuf[:, sl * 4096 + kk * 256:sl * 4096 + (kk + 2) * 256]
                            .rearrange("p (n z) -> p n z", n=2),
                            ag1_out[rows(kk):rows(kk) + 256, o * 256:(o + 1) * 256]
                            .rearrange("(n p) z -> p n z", p=128),
                            waits=[wslot_free[sl], dag1, rd(wbuf)]
                            if kk == 0 else [])
                    dmm = None
                    for t in range(NT):
                        bank = t % 4
                        b = t * 12
                        for k in range(16):
                            dmm = comp("tensor", lambda E, k=k, t=t, bank=bank,
                                       sl=sl: E.matmul(
                                pb[bank][:, 0:256],
                                fm[:, k * 1024 + t * 128:k * 1024 + (t + 1) * 128],
                                wbuf[:, sl * 4096 + k * 256:sl * 4096 + (k + 1) * 256],
                                start=(k == 0), stop=(k == 15)),
                                waits=[dwl, bank_free[bank], afm_all,
                                       dcoefs[t][0]] if k == 0 else [])
                        bank_fill[bank] = dmm
                        last_mm12 = dmm
                        if w12 == 0:
                            dsil = comp("scalar", lambda E, bank=bank, b=b, t=t:
                                        E.activation(
                                qst[:, t * 256:(t + 1) * 256],
                                pb[bank][:, 0:256], AF.Silu,
                                scale=st[:, b + 6:b + 7]),
                                waits=[bank_fill[bank], hsa_free[t]])
                            bank_free[bank] = dsil
                            hsa_free[t] = dsil
                        else:
                            hslot = (o * 8 + t) % 8
                            dh = comp("vector", lambda E, bank=bank, b=b, t=t,
                                      hslot=hslot:
                                      E.scalar_tensor_tensor(
                                hstg[:, hslot * 256:hslot * 256 + 256],
                                pb[bank][:, 0:256], st[:, b + 7:b + 8],
                                qst[:, t * 256:(t + 1) * 256],
                                ALU.mult, ALU.mult),
                                waits=[bank_fill[bank], hsa_free[t],
                                       hslot_free[hslot]])
                            bank_free[bank] = dh
                            hsa_free[t] = dh
                            dv = comp("vector", lambda E, hslot=hslot:
                                      E.reduce_max(
                                sv[:, 0:1],
                                hstg[:, hslot * 256:hslot * 256 + 256],
                                AX, apply_absolute_value=True), waits=[dh])
                            dv = comp("vector", lambda E, b=b: E.tensor_tensor(
                                st[:, b + 2:b + 3], st[:, b + 2:b + 3],
                                sv[:, 0:1], ALU.max), waits=[dv, dcoefs[t][1]])
                            dg = comp("scalar", lambda E, hslot=hslot,
                                      t=t: E.activation(
                                aux[:, 0:256],
                                hstg[:, hslot * 256:hslot * 256 + 256],
                                AF.Square,
                                accum_out=sv[:, 4 + (t % 4):5 + (t % 4)]),
                                waits=[dh, rd(aux)])
                            setrd(aux, dg)
                            dg = comp("vector", lambda E, b=b, t=t: E.tensor_add(
                                st[:, b + 3:b + 4], st[:, b + 3:b + 4],
                                sv[:, 4 + (t % 4):5 + (t % 4)]),
                                waits=[dg, dcoefs[t][2]])
                            dsp = dma("scalar", f"hs{hslot}",
                                      h_dram[t * 128:(t + 1) * 128,
                                             o * 256:(o + 1) * 256],
                                      hstg[:, hslot * 256:hslot * 256 + 256],
                                      waits=[dh, dv, dg])
                            hslot_free[hslot] = dsp
                    wslot_free[sl] = dmm
            setrd(fm, last_mm12)
            setrd(wbuf, last_mm12)
            hdeps = [None] * NT
            for t in range(NT):
                b = t * 12
                cS = lambda i, b=b: st[:, b + i:b + i + 1]
                last_stat = list(hslot_free) + [bank_free[i] for i in range(4)]
                d = comp("vector", lambda E, cS=cS: E.tensor_scalar(
                    cS(10), cS(3), 1.0 / DFF, 1e-6, ALU.mult, ALU.add),
                    waits=last_stat)
                d = comp("scalar", lambda E, cS=cS: E.activation(
                    cS(10), cS(10), AF.Sqrt), waits=[d])
                d = comp("vector", lambda E, cS=cS: E.reciprocal(
                    cS(10), cS(10)), waits=[d])
                d = comp("vector", lambda E, cS=cS: E.tensor_mul(
                    cS(8), cS(10), cS(2)), waits=[d])
                d = comp("vector", lambda E, cS=cS: E.tensor_scalar(
                    cS(8), cS(8), 1e-5, None, ALU.max), waits=[d])
                d = comp("vector", lambda E, cS=cS: E.reciprocal(
                    cS(9), cS(8)), waits=[d])
                d = comp("vector", lambda E, cS=cS: E.tensor_mul(
                    cS(4), cS(10), cS(9)), waits=[d])
                d = comp("vector", lambda E, cS=cS: E.tensor_scalar(
                    cS(4), cS(4), 127.0, None, ALU.mult), waits=[d])
                d = comp("vector", lambda E, cS=cS: E.scalar_tensor_tensor(
                    cS(6), cS(8), 0.5, bc[:, 2:3], ALU.mult, ALU.mult),
                    waits=[d])
                hdeps[t] = d
            w3sl = [None] * 6
            dwb = None
            nb = 0
            for g in range(NT // 4):
                ts4 = [4 * g + j for j in range(4)]
                dq3s = [None] * 4
                for ti, t in enumerate(ts4):
                    b = t * 12
                    dq3 = None
                    for hv in range(2):
                        drl = dma("scalar", "hrl", big3[:],
                                  h_dram[t * 128:(t + 1) * 128,
                                         hv * 4096:(hv + 1) * 4096],
                                  waits=list(hslot_free) + [rd(big3)])
                        for pch in range(2):
                            pc = hv * 2 + pch
                            du = comp("scalar", lambda E, pch=pch, b=b:
                                      E.activation(
                                sqs[:], big3[:, pch * 2048:(pch + 1) * 2048],
                                AF.Copy, bias=MAGIC, scale=st[:, b + 4:b + 5]),
                                waits=[drl, hdeps[t], rd(sqs)])
                            du = comp("vector", lambda E: E.tensor_scalar(
                                sqs[:], sqs[:], MAGIC, 127.0,
                                ALU.subtract, ALU.min), waits=[du])
                            du = comp("vector", lambda E: E.tensor_scalar(
                                aux[:], sqs[:], -128.0, None, ALU.max),
                                waits=[du, rd(aux)])
                            setrd(sqs, du)
                            dq3 = transpose16(
                                aux, du, fm_dst(fm, 512, pc * 16, ti * 128),
                                extra_waits=[rd(fm)])
                            setrd(aux, dq3)
                        setrd(big3, dq3)
                    dq3s[ti] = dq3
                dx0, s0 = load_xtile(ts4[0])
                dx1, s1 = load_xtile(ts4[1])
                dx2 = dma("sync", "xl2", qraw[:],
                          x_dram[ts4[2] * 128:(ts4[2] + 1) * 128, :],
                          waits=[rd(qraw), phase_entry[0]])
                dx3 = dma("sync", "xl3", hstg[:],
                          x_dram[ts4[3] * 128:(ts4[3] + 1) * 128, :],
                          waits=[rd(hstg), phase_entry[0]] + list(hslot_free))
                xts4 = [xt[s0], xt[s1], qraw, hstg]
                dxs4 = [dx0, dx1, dx2, dx3]
                evdep = [[None] * 4 for _ in range(4)]
                dlast_fm = None
                for p in range(4):
                    banks = [(nb + j) % 6 for j in range(4)]
                    nb += 4
                    dlast = None
                    for i in range(64):
                        li = p * 64 + i
                        ssl = li % 6
                        dwd = dma("sync", f"w3a{ssl}",
                                  w3b[:, ssl * 512:(ssl + 1) * 512],
                                  ag2_out[w3t_rows(i):w3t_rows(i) + 128,
                                          p * 512:(p + 1) * 512],
                                  waits=[w3sl[ssl], dag2])
                        for ti in range(4):
                            bk = banks[ti]
                            dlast = comp("tensor", lambda E, i=i, ti=ti, bk=bk,
                                         ssl=ssl: E.matmul(
                                pb[bk][:],
                                fm[:, i * 512 + ti * 128:i * 512 + (ti + 1) * 128],
                                w3b[:, ssl * 512:(ssl + 1) * 512],
                                start=(i == 0), stop=(i == 63)),
                                waits=([dwd] + dq3s + [bank_free[b_]
                                                       for b_ in banks])
                                if (i == 0 and ti == 0) else
                                ([dwd] if ti == 0 else []))
                        w3sl[ssl] = dlast
                    dlast_fm = dlast
                    for ti in range(4):
                        bk = banks[ti]
                        bank_fill[bk] = dlast
                        b = ts4[ti] * 12
                        dv = comp("vector", lambda E, bk=bk, ti=ti, p=p, b=b:
                                  E.scalar_tensor_tensor(
                            xts4[ti][:, p * 512:(p + 1) * 512], pb[bk][:],
                            st[:, b + 6:b + 7],
                            xts4[ti][:, p * 512:(p + 1) * 512],
                            ALU.mult, ALU.add),
                            waits=[bank_fill[bk], dxs4[ti]])
                        bank_free[bk] = dv
                        evdep[ti][p] = dv
                for ti in range(4):
                    t = ts4[ti]
                    dwb = dma("scalar", wsem, dst[t * 128:(t + 1) * 128, :],
                              xts4[ti][:], waits=evdep[ti] + [dwb])
                    if ti == 0:
                        xt_free[s0] = dwb
                    elif ti == 1:
                        xt_free[s1] = dwb
                setrd(qraw, dwb)
                setrd(hstg, dwb)
                setrd(fm, dlast_fm)
            return

        hslot_free_attn = [None] * 8
        ATT = {"gate_w": [None, None], "proj_w": [None, None],
               "qkv_done": None, "q_done": None}

        def attn_phase():
            dq = None
            for t in range(NT):
                dxt, slot = load_xtile(t)
                deps = ln_stats(t, slot, dxt, False)
                b = t * 12
                cS = lambda i, b=b: st[:, b + i:b + i + 1]
                d = comp("vector", lambda E, cS=cS: E.tensor_scalar(
                    cS(9), cS(0), 1.0 / C, None, ALU.mult), waits=deps)
                d = comp("vector", lambda E, cS=cS: E.tensor_scalar(
                    cS(10), cS(1), 1.0 / C, None, ALU.mult), waits=[d])
                d = comp("vector", lambda E, cS=cS: E.scalar_tensor_tensor(
                    cS(11), cS(9), 1.0, cS(9), ALU.mult, ALU.mult), waits=[d])
                d = comp("vector", lambda E, cS=cS: E.tensor_sub(
                    cS(10), cS(10), cS(11)), waits=[d])
                d = comp("vector", lambda E, cS=cS: E.tensor_scalar(
                    cS(10), cS(10), 1.0, 1e-5, ALU.mult, ALU.add), waits=[d])
                d = comp("scalar", lambda E, cS=cS: E.activation(
                    cS(10), cS(10), AF.Sqrt), waits=[d])
                d = comp("vector", lambda E, cS=cS: E.reciprocal(
                    cS(10), cS(10)), waits=[d])
                d = comp("vector", lambda E, cS=cS: E.scalar_tensor_tensor(
                    cS(11), cS(9), -1.0, cS(10), ALU.mult, ALU.mult), waits=[d])
                dxl = comp("scalar", lambda E, slot=slot, cS=cS: E.activation(
                    aux[:], xt[slot][:], AF.Identity, bias=cS(11), scale=cS(10)),
                    waits=[d, rd(aux)])
                dq = transpose16(aux, dxl, fm_dst(fm, 1024, 0, t * 128),
                                 extra_waits=[rd(fm)])
                setrd(aux, dq)
                xt_free[slot] = dxl

            wslot_free = [None, None]
            kin_done = []
            vin_done = []
            csb_dep = [None] * 3
            fc_list = [("k", i) for i in range(8)] + [("v", i) for i in range(8)] \
                + [("q", i) for i in range(8)]
            dmmlast = None
            dagkv = [None, None]
            for kind, i in fc_list:
                fc = {"q": 0, "k": 8, "v": 16}[kind] + i
                sl = fc % 2
                fcol = {"q": 0, "k": C, "v": 2 * C}[kind] + i * 256
                dL0 = dma("sync", "qw0",
                          big3[:, 0:2048].rearrange("p (n f) -> p n f", n=8),
                          wqkv[0:1024, fcol:fcol + 256]
                          .rearrange("(n p) f -> p n f", p=128),
                          waits=[rd(big3)])
                dL1 = dma("scalar", "qw1",
                          big3[:, 2048:4096].rearrange("p (n f) -> p n f", n=8),
                          wqkv[1024:2048, fcol:fcol + 256]
                          .rearrange("(n p) f -> p n f", p=128),
                          waits=[rd(big3)])
                dc0 = comp("scalar", lambda E, sl=sl: E.activation(
                    wbuf[:, sl * 4096:sl * 4096 + 2048], big3[:, 0:2048],
                    AF.Copy),
                    waits=[dL0, wslot_free[sl], rd(wbuf)])
                dwd = comp("scalar", lambda E, sl=sl: E.activation(
                    wbuf[:, sl * 4096 + 2048:(sl + 1) * 4096],
                    big3[:, 2048:4096], AF.Copy), waits=[dL1])
                setrd(big3, dwd)
                for t in range(NT):
                    bank = t % 3
                    dmm = None
                    for k in range(16):
                        dmm = comp("tensor", lambda E, k=k, t=t, bank=bank,
                                   sl=sl: E.matmul(
                            pb[bank][:, 0:256],
                            fm[:, k * 1024 + t * 128:k * 1024 + (t + 1) * 128],
                            wbuf[:, sl * 4096 + k * 256:sl * 4096 + (k + 1) * 256],
                            start=(k == 0), stop=(k == 15)),
                            waits=[dwd, bank_free[bank], dq] if k == 0 else [])
                    bank_fill[bank] = dmm
                    dmmlast = dmm
                    if kind == "v":
                        dvc = comp("vector", lambda E, bank=bank: E.tensor_copy(
                            smb[:, 0:256], pb[bank][:, 0:256]),
                            waits=[bank_fill[bank], rd(smb)])
                        dst_v = v_in.rearrange("(h tt) dd -> tt h dd", h=H)[
                            t * 128:(t + 1) * 128, i * 2:(i + 1) * 2, :]
                        dvw = dma("scalar", "kvw", dst_v,
                                  smb[:, 0:256]
                                  .rearrange("p (n dd) -> p n dd", n=2),
                                  waits=[dvc])
                        setrd(smb, dvw)
                        vin_done.append(dvw)
                        bank_free[bank] = dvc
                        continue
                    dcp = comp("scalar", lambda E, bank=bank, t=t: E.activation(
                        qraw[:, t * 256:(t + 1) * 256], pb[bank][:, 0:256],
                        AF.Copy), waits=[bank_fill[bank], rd(qraw)])
                    bank_free[bank] = dcp
                    csl = t % 3
                    lcol = i * 256
                    dc1 = dma("sync", f"csm{csl}", csb[:, csl * 512:csl * 512 + 256],
                              cosb[t * 128:(t + 1) * 128, lcol:lcol + 256],
                              waits=[csb_dep[csl]])
                    dc2 = dma("sync", f"csm{csl}", csb[:, csl * 512 + 256:csl * 512 + 512],
                              sinb[t * 128:(t + 1) * 128, lcol:lcol + 256],
                              waits=[dc1])
                    d1 = comp("vector", lambda E, t=t, csl=csl: E.tensor_mul(
                        qst[:, 0:256], qraw[:, t * 256:(t + 1) * 256],
                        csb[:, csl * 512:csl * 512 + 256]),
                        waits=[dc2, dcp, rd(qst)])
                    ybv = qraw[:, t * 256:(t + 1) * 256].rearrange(
                        "p (n two) -> p n two", two=2)
                    d2 = comp("vector", lambda E, ybv=ybv, csl=csl: E.tensor_mul(
                        qst[:, 256:512].rearrange("p (n two) -> p n two", two=2),
                        ybv[:, :, ::-1],
                        csb[:, csl * 512 + 256:csl * 512 + 512]
                        .rearrange("p (n two) -> p n two", two=2)), waits=[d1])
                    d3 = comp("vector", lambda E: E.tensor_add(
                        qst[:, 0:256], qst[:, 0:256], qst[:, 256:512]),
                        waits=[d2])
                    setrd(qst, d3)
                    setrd(qraw, d2)
                    csb_dep[csl] = d2
                    for j in range(2):
                        hh = i * 2 + j
                        scol = (qde[:, hh * 4 + (t % 4):hh * 4 + (t % 4) + 1]
                                if kind == "q" else kde[:, hh:hh + 1])
                        d3 = comp("vector", lambda E, j=j, scol=scol:
                                  E.tensor_scalar(
                            aux[:, j * 128:(j + 1) * 128],
                            qst[:, j * 128:(j + 1) * 128], scol, None, ALU.mult),
                            waits=[d3, rd(aux), dep_consts] if j == 0 else [d3])
                    if kind == "q":
                        dtp = transpose16(
                            aux, d3, fm_dst(fm, 1024, 16 + i * 2, t * 128),
                            nt_=2, extra_waits=[rd(fm)])
                        setrd(aux, dtp)
                        ATT["q_done"] = dtp
                    else:
                        dtp = transpose16(
                            aux, d3,
                            lambda j0, n: smb[:, 256 + j0 * 128:
                                              256 + (j0 + n) * 128]
                            .rearrange("p (nn z) -> p nn z", z=128),
                            nt_=2, extra_waits=[rd(smb)])
                        setrd(aux, dtp)
                        dst_k = k_in.rearrange("(h dd) tt -> dd h tt", h=H)[
                            :, i * 2:(i + 1) * 2, t * 128:(t + 1) * 128]
                        dkw = dma("scalar", "kvw", dst_k,
                                  smb[:, 256:512]
                                  .rearrange("p (n tt) -> p n tt", n=2),
                                  waits=[dtp])
                        setrd(smb, dkw)
                        kin_done.append(dkw)
                wslot_free[sl] = dmmlast
                if i == 7 and kind == "k":
                    dagkv[0] = coll(lambda E: E.collective_compute(
                        "AllGather", ALU.bypass,
                        replica_groups=[[0, 1, 2, 3], [4, 5, 6, 7]],
                        ins=[k_in.opt()], outs=[k_out.opt()]),
                        waits=[kin_done[-1]])
                elif i == 7 and kind == "v":
                    dagkv[1] = coll(lambda E: E.collective_compute(
                        "AllGather", ALU.bypass,
                        replica_groups=[[0, 1, 2, 3], [4, 5, 6, 7]],
                        ins=[v_in.opt()], outs=[v_out.opt()]),
                        waits=[vin_done[-1]])
            ATT["qkv_done"] = dmmlast
            ATT["qkv_vdone"] = (vS, sch.n(vS))
            dagk, dagv = dagkv
            dep_kin = kin_done[-1]
            dep_vin = vin_done[-1]
            dzlast_h = [None]

            kslot_free = [None, None, None]
            for chunk in range(2):
                dgate = emit_gate(chunk)
                plist = [p for p, pl in enumerate(PASS_LIST) if pl[0] == chunk]
                np_ = len(plist)
                for h in range(H):
                    ypb = 3 + (h % 2)
                    nacc = 0
                    dav = None
                    for pi, p in enumerate(plist):
                        _, src = PASS_LIST[p]
                        ksl = pi % 3
                        if src[0] == "diag":
                            ksrc = k_in[h * 128:(h + 1) * 128,
                                        chunk * 512:(chunk + 1) * 512]
                            vsrc = v_in[h * TOK + chunk * 512:
                                        h * TOK + (chunk + 1) * 512, :]
                            kw = [kslot_free[ksl], dep_kin]
                            vw = [kslot_free[ksl], dep_vin]
                        else:
                            slot, sub = src[1], src[2]
                            ksrc = k_out[slot * 2048 + h * 128:
                                         slot * 2048 + (h + 1) * 128,
                                         sub * 512:(sub + 1) * 512]
                            vsrc = v_out[slot * (H * TOK) + h * TOK + sub * 512:
                                         slot * (H * TOK) + h * TOK
                                         + (sub + 1) * 512, :]
                            kw = [kslot_free[ksl], dagk]
                            vw = [kslot_free[ksl], dagv]
                        dk = dma("sync", f"klm{ksl}", kst[:, ksl * 512:(ksl + 1) * 512],
                                 ksrc, waits=kw)
                        dvv = dma("sync", f"vlm{ksl}",
                                  vst[:, ksl * 512:(ksl + 1) * 512]
                                  .rearrange("p (n dd) -> p n dd", n=4),
                                  vsrc.rearrange("(n p) dd -> p n dd", p=128),
                                  waits=vw)
                        for q in range(4):
                            sbank = q % 3
                            dsc = comp("tensor", lambda E, q=q, ksl=ksl, h=h,
                                       chunk=chunk, sbank=sbank: E.matmul(
                                pb[sbank][:],
                                kst[:, ksl * 512 + q * 128:
                                    ksl * 512 + (q + 1) * 128],
                                fm[:, 16384 + h * 1024 + chunk * 512:
                                   16384 + h * 1024 + (chunk + 1) * 512],
                                start=True, stop=True),
                                waits=[dk, bank_free[sbank], ATT["q_done"]])
                            bank_fill[sbank] = dsc
                            scol = sct[:, (h * NPASS + p) * 4 + q:
                                       (h * NPASS + p) * 4 + q + 1]
                            if src[0] == "diag":
                                dev = comp("vector", lambda E, q=q, sbank=sbank,
                                           scol=scol: E.scalar_tensor_tensor(
                                    smb[:, q * 512:(q + 1) * 512], pb[sbank][:],
                                    scol, tri[:, q * 512:(q + 1) * 512],
                                    ALU.mult, ALU.mult),
                                    waits=[bank_fill[sbank], rd(smb),
                                           dep_consts])
                            else:
                                dev = comp("scalar", lambda E, q=q, sbank=sbank,
                                           scol=scol: E.activation(
                                    smb[:, q * 512:(q + 1) * 512], pb[sbank][:],
                                    AF.Copy, scale=scol),
                                    waits=[bank_fill[sbank], rd(smb)])
                            bank_free[sbank] = dev
                            last_pass = (pi == np_ - 1)
                            for tt in range(4):
                                dav = comp("tensor", lambda E, q=q, tt=tt,
                                           ksl=ksl, ypb=ypb,
                                           first=(nacc == 0 and tt == 0),
                                           stp=(last_pass and q == 3 and tt == 3):
                                           E.matmul(
                                    pb[ypb][:, tt * 128:(tt + 1) * 128],
                                    smb[:, q * 512 + tt * 128:
                                        q * 512 + (tt + 1) * 128],
                                    vst[:, ksl * 512 + q * 128:
                                        ksl * 512 + (q + 1) * 128],
                                    start=first, stop=stp,
                                    skip_group_check=True),
                                    waits=[dev, dvv, bank_free[ypb]]
                                    if tt == 0 else [])
                            setrd(smb, dav)
                            nacc += 1
                        kslot_free[ksl] = dav
                    bank_fill[ypb] = dav
                    for tt in range(4):
                        i4 = h * 4 + tt
                        comp("vector", lambda E, i4=i4, tt=tt, ypb=ypb:
                             E.reduce_sum(st[:, 96 + i4:97 + i4],
                                          pb[ypb][:, tt * 128:(tt + 1) * 128],
                                          AX), waits=[bank_fill[ypb]])
                    vred = (vS, sch.n(vS))
                    dstat = None
                    for tt in range(4):
                        i4 = h * 4 + tt
                        dstat = comp("scalar", lambda E, i4=i4, tt=tt, ypb=ypb:
                                     E.activation(
                            sqs[:, 0:128], pb[ypb][:, tt * 128:(tt + 1) * 128],
                            AF.Square, accum_out=st[:, 160 + i4:161 + i4]),
                            waits=[bank_fill[ypb], rd(sqs), vred])
                        setrd(sqs, dstat)
                    c0 = 96 + 4 * h
                    d = comp("vector", lambda E, c0=c0: E.tensor_scalar(
                        st[:, c0 + 128:c0 + 132], st[:, c0:c0 + 4],
                        -1.0 / D, None, ALU.mult), waits=[vred])
                    d = comp("vector", lambda E, c0=c0: E.tensor_scalar(
                        st[:, c0 + 64:c0 + 68], st[:, c0 + 64:c0 + 68],
                        1.0 / D, None, ALU.mult), waits=[d, dstat])
                    d = comp("vector", lambda E, c0=c0: E.scalar_tensor_tensor(
                        st[:, c0 + 192:c0 + 196], st[:, c0 + 128:c0 + 132], 1.0,
                        st[:, c0 + 128:c0 + 132], ALU.mult, ALU.mult), waits=[d])
                    d = comp("vector", lambda E, c0=c0: E.tensor_sub(
                        st[:, c0 + 64:c0 + 68], st[:, c0 + 64:c0 + 68],
                        st[:, c0 + 192:c0 + 196]), waits=[d])
                    d = comp("vector", lambda E, c0=c0: E.tensor_scalar(
                        st[:, c0 + 64:c0 + 68], st[:, c0 + 64:c0 + 68],
                        1.0, 1e-5, ALU.mult, ALU.add), waits=[d])
                    d = comp("scalar", lambda E, c0=c0: E.activation(
                        st[:, c0 + 192:c0 + 196], st[:, c0 + 64:c0 + 68],
                        AF.Sqrt), waits=[d])
                    drs = comp("vector", lambda E, c0=c0: E.reciprocal(
                        st[:, c0 + 192:c0 + 196], st[:, c0 + 192:c0 + 196]),
                        waits=[d])
                    dza = None
                    for tt in range(4):
                        i4 = h * 4 + tt
                        gsl = g32sl(tt, h)
                        zsl = gch[:, tt * 2048 + h * 128:tt * 2048 + (h + 1) * 128]
                        dza = comp("vector", lambda E, tt=tt, ypb=ypb, i4=i4:
                                   E.tensor_scalar(
                            qst[:, 0:128], pb[ypb][:, tt * 128:(tt + 1) * 128],
                            st[:, 224 + i4:225 + i4], st[:, 288 + i4:289 + i4],
                            ALU.add, ALU.mult), waits=[drs, dgate, rd(qst)])
                        d1 = comp("vector", lambda E, gsl=gsl: E.tensor_mul(
                            qst[:, 128:256], qst[:, 0:128], gsl),
                            waits=[dza])
                        dt1 = comp("tensor", lambda E: E.transpose(
                            pb[5][:, 0:128], qst[:, 128:256], idnf[:]),
                            waits=[d1, bank_free[5], dep_consts])
                        dt2 = comp("tensor", lambda E, gsl=gsl: E.transpose(
                            pb[5][:, 128:256], gsl, idnf[:]), waits=[dt1])
                        setrd(qst, dt1)
                        setrd(g32buf(tt, h), dt2)
                        bank_fill[5] = dt2
                        ev1 = comp("vector", lambda E, h=h: E.tensor_scalar(
                            sqs[:, 0:128], pb[5][:, 128:256],
                            gnb[:, h:h + 1], None, ALU.mult),
                            waits=[dt2, rd(sqs)])
                        ev2 = comp("vector", lambda E, h=h, zsl=zsl:
                                   E.scalar_tensor_tensor(
                            zsl, pb[5][:, 0:128], gnw[:, h:h + 1],
                            sqs[:, 0:128], ALU.mult, ALU.add),
                            waits=[ev1, rd(gch)])
                        setrd(sqs, ev2)
                        bank_free[5] = ev2
                        dzlast_h[0] = ev2
                    bank_free[ypb] = dza
                    if chunk == 1 and h == H - 1:
                        setrd(fm, dav)
                attn_tail(chunk, dzlast_h[0])
            for i_ in range(8):
                hslot_free_attn[i_] = rd(hstg)
            return

        def g32buf(tt, h):
            if tt in (0, 1):
                return big3
            if tt == 2:
                return qraw
            return hstg

        def g32sl(tt, h, w=1):
            if tt == 0:
                return big3[:, h * 128:(h + w) * 128]
            if tt == 1:
                return big3[:, 2048 + h * 128:2048 + (h + w) * 128]
            if tt == 2:
                return qraw[:, h * 128:(h + w) * 128]
            return hstg[:, h * 128:(h + w) * 128]

        def emit_gate(chunk):
            wslot_free = ATT["gate_w"]
            dsg = None
            for gc in range(8):
                sl = gc % 2
                dwd = dma("gpsimd", f"gwl{sl}",
                          wbuf[:, sl * 4096:(sl + 1) * 4096]
                          .rearrange("p (n f) -> p n f", n=16),
                          wgd[:, gc * 256:(gc + 1) * 256]
                          .rearrange("(n p) f -> p n f", p=128),
                          waits=[wslot_free[sl], rd(wbuf), ATT["qkv_done"],
                                 rd(gch)])
                for tt in range(4):
                    t = chunk * 4 + tt
                    bank = tt % 3
                    dmm = None
                    for k in range(16):
                        dmm = comp("tensor", lambda E, k=k, t=t, bank=bank,
                                   sl=sl: E.matmul(
                            pb[bank][:, 0:256],
                            fm[:, k * 1024 + t * 128:
                               k * 1024 + (t + 1) * 128],
                            wbuf[:, sl * 4096 + k * 256:
                                 sl * 4096 + (k + 1) * 256],
                            start=(k == 0), stop=(k == 15)),
                            waits=[dwd, bank_free[bank]] if k == 0 else [])
                    bank_fill[bank] = dmm
                    dsg = comp("scalar", lambda E, bank=bank, tt=tt, gc=gc:
                               E.activation(
                        g32sl(tt, gc * 2, 2),
                        pb[bank][:, 0:256], AF.Silu),
                        waits=[bank_fill[bank], rd(big3), rd(qraw),
                               rd(hstg), ATT.get("qkv_vdone")]
                        + list(hslot_free_attn)
                        if gc == 0 and tt == 0 else
                        [bank_fill[bank]])
                    bank_free[bank] = dsg
                    wslot_free[sl] = dmm
                    setrd(wbuf, dmm)
            return dsg

        def attn_tail(chunk, dzlast):
            wslot2 = ATT["proj_w"]
            dwb = None
            for half in range(2):
                dxts = [None, None]
                dadds = [None, None]
                for pc in range(8):
                    sl = pc % 2
                    dwd = dma("gpsimd", f"gwl{sl}",
                              wbuf[:, sl * 4096:(sl + 1) * 4096]
                              .rearrange("p (n f) -> p n f", n=16),
                              wpd[:, pc * 256:(pc + 1) * 256]
                              .rearrange("(n p) f -> p n f", p=128),
                              waits=[wslot2[sl], rd(wbuf)])
                    for tt2 in range(2):
                        tt = half * 2 + tt2
                        t = chunk * 4 + tt
                        if pc == 0:
                            dxts[tt2] = load_xtile(t)
                        bank = (pc * 2 + tt2) % 3
                        dmm = None
                        for hh in range(16):
                            dmm = comp("tensor", lambda E, hh=hh, tt=tt,
                                       bank=bank, sl=sl: E.matmul(
                                pb[bank][:, 0:256],
                                gch[:, tt * 2048 + hh * 128:
                                    tt * 2048 + (hh + 1) * 128],
                                wbuf[:, sl * 4096 + hh * 256:
                                     sl * 4096 + (hh + 1) * 256],
                                start=(hh == 0), stop=(hh == 15)),
                                waits=[dwd, bank_free[bank], dzlast]
                                if hh == 0 else [])
                        bank_fill[bank] = dmm
                        setrd(gch, dmm)
                        setrd(wbuf, dmm)
                        slot = dxts[tt2][1]
                        dadds[tt2] = comp("vector",
                                          lambda E, bank=bank, pc=pc, slot=slot:
                                          E.scalar_tensor_tensor(
                            xt[slot][:, pc * 256:(pc + 1) * 256],
                            pb[bank][:, 0:256], 1.0,
                            xt[slot][:, pc * 256:(pc + 1) * 256],
                            ALU.mult, ALU.add),
                            waits=[bank_fill[bank], dxts[tt2][0], dadds[tt2]])
                        bank_free[bank] = dadds[tt2]
                        wslot2[sl] = dmm
                for tt2 in range(2):
                    tt = half * 2 + tt2
                    t = chunk * 4 + tt
                    slot = dxts[tt2][1]
                    dwb = dma("scalar", "xw2", x_dram[t * 128:(t + 1) * 128, :],
                              xt[slot][:], waits=[dadds[tt2], dwb])
                    xt_free[slot] = dwb
            return

        if DO_FFN1:
            ffn_phase(x_dram, "xw1")
            phase_entry[0] = (SM["xw1"], sch.n(SM["xw1"]))
        if DO_ATTN:
            attn_phase()
            phase_entry[0] = (SM["xw2"], sch.n(SM["xw2"]))
        if DO_FFN2:
            ffn_phase(out_c, "xw3")
        else:
            dma("gpsimd", "xw3", out_c, x_dram,
                waits=[phase_entry[0]])
        if DBG:
            dma("sync", "cst", dbg_st[:, 0:96], st[:, 0:96],
                waits=[(vS, sch.n(vS)), (sS, sch.n(sS)), (tS, sch.n(tS))])
            dma("sync", "cst", dbg_row[0:1, 128:135], srow[0:1, 128:135])
            dma("sync", "cst", dbg_bc[:, 0:6], bc[:, 0:6])
            dma("sync", "cst", dbg_afm, fm[:, 0:16384])
            dma("gpsimd", "gdbg", dbg_h, h_dram[0:128, :])
        final_waits = [(s, sch.n(s)) for s in SM.values()]
        final_waits += [(vS, sch.n(vS)), (sS, sch.n(sS)), (tS, sch.n(tS)),
                        (gS, sch.n(gS)), (ccS, sch.n(ccS))]
        sch.add("sync", lambda E: E.nop(), waits=final_waits)

        @block.sync
        def _(E):
            sch.replay("sync", E)

        @block.gpsimd
        def _(E):
            sch.replay("gpsimd", E)

        @block.vector
        def _(E):
            sch.replay("vector", E)

        @block.scalar
        def _(E):
            sch.replay("scalar", E)

        @block.tensor
        def _(E):
            sch.replay("tensor", E)

    return nc


def kernel(L0_attn_x, L0_attn_w_qkv, L0_attn_w_g, L0_attn_w_proj,
           L0_attn_gn_weight, L0_attn_gn_bias, L0_ffn_w1, L0_ffn_w2, L0_ffn_w3):
    x = np.asarray(L0_attn_x, dtype=np.float32)
    shared = _shared_inputs(np.asarray(L0_attn_w_qkv, np.float32),
                            np.asarray(L0_attn_w_g, np.float32),
                            np.asarray(L0_attn_w_proj, np.float32),
                            np.asarray(L0_attn_gn_weight, np.float32),
                            np.asarray(L0_attn_gn_bias, np.float32))
    in_maps = [
        _host_inputs(core, x,
                     np.asarray(L0_ffn_w1, np.float32),
                     np.asarray(L0_ffn_w2, np.float32),
                     np.asarray(L0_ffn_w3, np.float32), shared)
        for core in range(NCORES)
    ]
    nc = build_nc()
    res = run_bass_kernel_spmd(nc, in_maps, core_ids=list(range(NCORES)))
    out = np.zeros((B, T, C), np.float32)
    for core in range(NCORES):
        b = core // 4
        cA, cB = _chunks_of(core)
        oc = np.asarray(res.results[core]["out_c"])
        out[b, cA * 512:(cA + 1) * 512] = oc[0:512]
        out[b, cB * 512:(cB + 1) * 512] = oc[512:1024]
    return out


# version 09
# speedup vs baseline: 1.0288x; 1.0288x over previous
import math
import numpy as np
import ml_dtypes
import concourse.bass as bass
import concourse.mybir as mybir
from concourse.bass_utils import run_bass_kernel_spmd
from contextlib import ExitStack

F32 = mybir.dt.float32
BF16 = mybir.dt.bfloat16
ALU = mybir.AluOpType
AF = mybir.ActivationFunctionType
AX = mybir.AxisListType.X

B, T, C, H, D, DFF = 2, 4096, 2048, 16, 128, 8192
NCORES = 8
NT = 8
TOK = NT * 128
MAGIC = float(np.float32(1.5 * 2 ** 23))
SCALE_BASE = 512.0
NELEM_W = float(DFF * C)

DO_FFN1 = True
DBG = False
DO_ATTN = True
DO_FFN2 = True

PASS_LIST = [
    (0, ("diag",)), (0, ("slot", 0, 0)), (0, ("slot", 1, 0)), (0, ("slot", 2, 0)),
    (1, ("diag",)), (1, ("slot", 0, 0)), (1, ("slot", 1, 0)), (1, ("slot", 2, 0)),
    (1, ("slot", 3, 0)), (1, ("slot", 3, 1)), (1, ("slot", 2, 1)), (1, ("slot", 1, 1)),
]
NPASS = len(PASS_LIST)


def _gammas():
    return 1.0 - 2.0 ** (-5.0 - np.arange(H, dtype=np.float64))


def _chunks_of(core):
    r = core % 4
    return r, 7 - r


def _xpos_tables(positions):
    dim_scale = (np.arange(0, C, 2, dtype=np.float64) + 0.4 * C) / (1.4 * C)
    pos = positions.astype(np.float64)
    scale = dim_scale[None, :] ** (pos[:, None] / SCALE_BASE)
    inv_freq = 1.0 / 10000.0 ** (np.arange(0, C, 2, dtype=np.float64) / C)
    sinu = pos[:, None] * inv_freq[None, :]
    sin = np.repeat(np.sin(sinu) * scale, 2, axis=-1)
    cos = np.repeat(np.cos(sinu) * scale, 2, axis=-1)
    bs = sin.copy()
    bs[:, 0::2] = -sin[:, 0::2]
    return cos, bs


def _sctab_for_core(core):
    cA, cB = _chunks_of(core)
    gam = _gammas()
    tab = np.zeros((H, NPASS, 4), dtype=np.float64)
    for p, (qc, src) in enumerate(PASS_LIST):
        c_own = cA if qc == 0 else cB
        tc0 = c_own * 512
        if src[0] == "diag":
            s_chunk, valid = c_own, True
        else:
            slot, sub = src[1], src[2]
            s_chunk = slot if sub == 0 else 7 - slot
            valid = s_chunk < c_own
        if not valid:
            continue
        for q in range(4):
            s0 = s_chunk * 512 + q * 128
            for h in range(H):
                tab[h, p, q] = gam[h] ** (tc0 - s0)
    with np.errstate(under="ignore"):
        flat = tab.reshape(1, -1).astype(np.float32)
    return np.ascontiguousarray(np.broadcast_to(flat, (128, flat.shape[1])))


def _shared_inputs(w_qkv, w_g, w_proj, gnw, gnb):
    gam = _gammas()
    p_ = np.arange(128)
    qdc = np.zeros((128, H * 4), np.float64)
    for h in range(H):
        for j in range(4):
            qdc[:, h * 4 + j] = gam[h] ** (128 * j + p_) / math.sqrt(D)
    kdc = np.zeros((128, H), np.float64)
    for h in range(H):
        kdc[:, h] = gam[h] ** (-p_.astype(np.float64))
    tri = np.zeros((128, 4 * 512), np.float32)
    f_ = np.arange(512)
    for q in range(4):
        tri[:, q * 512:(q + 1) * 512] = (
            f_[None, :] >= (p_[:, None] + 128 * q)).astype(np.float32)
    return dict(
        wqkv=np.ascontiguousarray(w_qkv, np.float32),
        wg=np.ascontiguousarray(w_g, np.float32),
        wproj=np.ascontiguousarray(w_proj, np.float32),
        gnw_r=np.ascontiguousarray(gnw.reshape(H, 128).T, np.float32),
        gnb_r=np.ascontiguousarray(gnb.reshape(H, 128).T, np.float32),
        qdec=qdc.astype(np.float32),
        kdec=kdc.astype(np.float32),
        tri=tri.astype(ml_dtypes.bfloat16),
        ident=np.eye(128, dtype=ml_dtypes.bfloat16),
        identf=np.eye(128, dtype=np.float32),
        ones_r=np.ones((1, 128), np.float32),
    )


def _host_inputs(core, x, w1, w2, w3, shared):
    b = core // 4
    cA, cB = _chunks_of(core)
    xs = np.concatenate(
        [x[b, cA * 512:(cA + 1) * 512], x[b, cB * 512:(cB + 1) * 512]], axis=0)
    pos = np.concatenate(
        [np.arange(cA * 512, (cA + 1) * 512), np.arange(cB * 512, (cB + 1) * 512)])
    cos, bs = _xpos_tables(pos)
    d = dict(
        x_in=np.ascontiguousarray(xs, dtype=np.float32),
        w1s=np.ascontiguousarray(w1.T[core * 256:(core + 1) * 256]),
        w2s=np.ascontiguousarray(w2.T[core * 256:(core + 1) * 256]),
        w3s=np.ascontiguousarray(w3.T[core * 1024:(core + 1) * 1024]),
        cosb=cos.astype(np.float32),
        sinb=bs.astype(np.float32),
        sctab=_sctab_for_core(core),
    )
    d.update(shared)
    return d


class Sched:
    def __init__(self):
        self.steps = []
        self.cnt = {}

    def n(self, sem):
        return self.cnt.get(id(sem), 0)

    def add(self, eng, fn, waits=(), inc=None, incv=1):
        if inc is not None:
            self.cnt[id(inc)] = self.n(inc) + incv
        seen = {}
        for w in waits:
            if w is None:
                continue
            s, v = w
            if id(s) in seen:
                seen[id(s)] = (s, max(seen[id(s)][1], v))
            else:
                seen[id(s)] = (s, v)
        self.steps.append((eng, fn, list(seen.values()),
                           [(inc, incv)] if inc is not None else []))

    def replay(self, name, eng):
        for en, fn, waits, incs in self.steps:
            if en != name:
                continue
            for sem, v in waits:
                eng.wait_ge(sem, v)
            instr = fn(eng)
            for sem, v in incs:
                instr.then_inc(sem, v)


def build_nc():
    nc = bass.Bass("TRN2", target_bir_lowering=False)
    ap = lambda t: t.ap()

    x_in = ap(nc.dram_tensor("x_in", [TOK, C], F32, kind="ExternalInput"))
    wqkv = ap(nc.dram_tensor("wqkv", [C, 3 * C], F32, kind="ExternalInput"))
    wgd = ap(nc.dram_tensor("wg", [C, C], F32, kind="ExternalInput"))
    wpd = ap(nc.dram_tensor("wproj", [C, C], F32, kind="ExternalInput"))
    gnw_in = ap(nc.dram_tensor("gnw_r", [128, H], F32, kind="ExternalInput"))
    gnb_in = ap(nc.dram_tensor("gnb_r", [128, H], F32, kind="ExternalInput"))
    w1s = ap(nc.dram_tensor("w1s", [256, DFF], F32, kind="ExternalInput"))
    w2s = ap(nc.dram_tensor("w2s", [256, DFF], F32, kind="ExternalInput"))
    w3s = ap(nc.dram_tensor("w3s", [1024, C], F32, kind="ExternalInput"))
    cosb = ap(nc.dram_tensor("cosb", [TOK, C], F32, kind="ExternalInput"))
    sinb = ap(nc.dram_tensor("sinb", [TOK, C], F32, kind="ExternalInput"))
    qdec = ap(nc.dram_tensor("qdec", [128, H * 4], F32, kind="ExternalInput"))
    kdec = ap(nc.dram_tensor("kdec", [128, H], F32, kind="ExternalInput"))
    tri_in = ap(nc.dram_tensor("tri", [128, 4 * 512], BF16, kind="ExternalInput"))
    sctab_in = ap(nc.dram_tensor("sctab", [128, H * NPASS * 4], F32,
                                 kind="ExternalInput"))
    ident_in = ap(nc.dram_tensor("ident", [128, 128], BF16, kind="ExternalInput"))
    identf_in = ap(nc.dram_tensor("identf", [128, 128], F32, kind="ExternalInput"))
    ones_in = ap(nc.dram_tensor("ones_r", [1, 128], F32, kind="ExternalInput"))

    out_c = ap(nc.dram_tensor("out_c", [TOK, C], F32, kind="ExternalOutput"))
    if DBG:
        dbg_st = ap(nc.dram_tensor("dbg_st", [128, 384], F32, kind="ExternalOutput"))
        dbg_row = ap(nc.dram_tensor("dbg_row", [1, 144], F32, kind="ExternalOutput"))
        dbg_bc = ap(nc.dram_tensor("dbg_bc", [128, 8], F32, kind="ExternalOutput"))
        dbg_afm = ap(nc.dram_tensor("dbg_afm", [128, 16384], BF16, kind="ExternalOutput"))
        dbg_h = ap(nc.dram_tensor("dbg_h", [128, DFF], F32, kind="ExternalOutput"))

    x_dram = ap(nc.dram_tensor("x_dram", [TOK, C], F32, kind="Internal"))
    h_dram = ap(nc.dram_tensor("h_dram", [TOK, DFF], F32, kind="Internal"))
    ag1_in = ap(nc.dram_tensor("ag1_in", [512, DFF], BF16, kind="Internal"))
    ag1_out = ap(nc.dram_tensor("ag1_out", [NCORES * 512, DFF], BF16, kind="Internal"))
    ag2_in = ap(nc.dram_tensor("ag2_in", [1024, C], BF16, kind="Internal"))
    ag2_out = ap(nc.dram_tensor("ag2_out", [NCORES * 1024, C], BF16, kind="Internal"))
    ar_in = ap(nc.dram_tensor("ar_in", [1, 4], F32, kind="Internal"))
    ar_out = ap(nc.dram_tensor("ar_out", [1, 4], F32, kind="Internal"))
    pflip = ap(nc.dram_tensor("pflip", [128, 1], F32, kind="Internal"))
    k_in = ap(nc.dram_tensor("k_in", [H * 128, TOK], BF16, kind="Internal"))
    k_out = ap(nc.dram_tensor("k_out", [4 * H * 128, TOK], BF16, kind="Internal"))
    v_in = ap(nc.dram_tensor("v_in", [H * TOK, D], BF16, kind="Internal"))
    v_out = ap(nc.dram_tensor("v_out", [4 * H * TOK, D], BF16, kind="Internal"))

    def w1t_rows(k):
        return (k // 2) * 512 + (k % 2) * 128

    def w2t_rows(k):
        return (k // 2) * 512 + (k % 2) * 128 + 256

    def w3t_rows(k):
        return (k // 8) * 1024 + (k % 8) * 128

    sch = Sched()

    with ExitStack() as ctx:
        e = ctx.enter_context
        xt = [e(nc.sbuf_tensor(f"xt{i}", [128, C], F32)) for i in range(2)]
        fm = e(nc.sbuf_tensor("fm", [128, 32768], BF16))
        wbuf = e(nc.sbuf_tensor("wbuf", [128, 8192], BF16))
        big3 = e(nc.sbuf_tensor("big3", [128, 4096], F32))
        aux = e(nc.sbuf_tensor("aux", [128, 2048], BF16))
        kst = e(nc.sbuf_tensor("kst", [128, 3 * 512], BF16))
        vst = e(nc.sbuf_tensor("vst", [128, 3 * 512], BF16))
        csb = e(nc.sbuf_tensor("csb", [128, 1536], F32))
        w3b = e(nc.sbuf_tensor("w3b", [128, 6144], BF16))
        hstg = e(nc.sbuf_tensor("hstg", [128, 2048], F32))
        smb = e(nc.sbuf_tensor("smb", [128, 4 * 512], BF16))
        qst = e(nc.sbuf_tensor("qst", [128, 2048], F32))
        gch = e(nc.sbuf_tensor("gch", [128, DFF], BF16))
        sqs = e(nc.sbuf_tensor("sqs", [128, C], F32))
        qraw = e(nc.sbuf_tensor("qraw", [128, 2048], F32))
        tri = e(nc.sbuf_tensor("tri_s", [128, 4 * 512], BF16))
        gnw = e(nc.sbuf_tensor("gnw_s", [128, H], F32))
        gnb = e(nc.sbuf_tensor("gnb_s", [128, H], F32))
        sct = e(nc.sbuf_tensor("sct_s", [128, H * NPASS * 4], F32))
        qde = e(nc.sbuf_tensor("qde_s", [128, H * 4], F32))
        kde = e(nc.sbuf_tensor("kde_s", [128, H], F32))
        idn = e(nc.sbuf_tensor("idn_s", [128, 128], BF16))
        idnf = e(nc.sbuf_tensor("idnf_s", [128, 128], F32))
        ones1 = e(nc.sbuf_tensor("ones1", [1, 128], F32))
        st = e(nc.sbuf_tensor("stats", [128, 384], F32))
        sv = e(nc.sbuf_tensor("sv", [128, 16], F32))
        srow = e(nc.sbuf_tensor("srow", [1, 144], F32))
        bc = e(nc.sbuf_tensor("bc", [128, 8], F32))

        pb = [e(nc.psum_tensor(f"pb{i}", [128, 512], F32)) for i in range(6)]
        ptr2 = [e(nc.psum_tensor(f"ptr{i}", [128, 1024], BF16)) for i in range(2)]

        sem_names = ["cst", "xcp", "bl", "p0f", "agw", "wl0", "wl1",
                     "xl0", "xl1", "hs0", "hs1", "hrl", "w3m0", "w3m1",
                     "w3t0", "w3t1", "csm0", "csm1", "csm2", "csm3", "hs2", "hs3",
                     "hs4", "hs5", "hs6", "hs7", "xl2", "xl3",
                     "w3a0", "w3a1", "w3a2", "w3a3", "w3a4", "w3a5",
                     "qw0", "qw1",
                     "klm0", "klm1", "klm2", "vlm0", "vlm1", "vlm2",
                     "kvw", "xw1", "xw2", "xw3", "gwl0", "gwl1", "gdbg"]
        SM = {nm: e(nc.semaphore(nm)) for nm in sem_names}
        vS = e(nc.semaphore("vS"))
        sS = e(nc.semaphore("sS"))
        tS = e(nc.semaphore("tS"))
        gS = e(nc.semaphore("gS"))
        ccS = e(nc.semaphore("ccS"))

        block = e(nc.Block())

        ESEM = {"vector": vS, "scalar": sS, "gpsimd": gS, "tensor": tS}

        def dma(queue, sem, out, in_, waits=()):
            s = SM[sem]
            sch.add(queue, lambda E, o=out, i=in_: E.dma_start(out=o, in_=i),
                    waits=waits, inc=s, incv=16)
            return (s, sch.n(s))

        def comp(eng, fn, waits=()):
            s = ESEM[eng]
            sch.add(eng, fn, waits=waits, inc=s)
            return (s, sch.n(s))

        def coll(fn, waits=()):
            sch.add("gpsimd", fn, waits=waits, inc=ccS, incv=1)
            return (ccS, sch.n(ccS))

        bank_fill = [(tS, 0)] * 6
        bank_free = [None] * 6
        ptr_fill = [(tS, 0)] * 2
        ptr_free = [None, None]
        buf_rd = {}

        def rd(buf):
            return buf_rd.get(id(buf))

        def setrd(buf, dep):
            old = buf_rd.get(id(buf))
            if old is not None and old[0] is dep[0]:
                dep = (dep[0], max(dep[1], old[1]))
            buf_rd[id(buf)] = dep

        for dst, src in ((tri, tri_in), (gnw, gnw_in), (gnb, gnb_in),
                         (sct, sctab_in), (qde, qdec), (kde, kdec),
                         (idn, ident_in), (idnf, identf_in),
                         (ones1, ones_in)):
            dma("sync", "cst", dst[:], src)
        dep_consts = (SM["cst"], sch.n(SM["cst"]))
        dma("gpsimd", "xcp", x_dram, x_in)
        dep_xcopy = (SM["xcp"], sch.n(SM["xcp"]))

        def p0_sum(slc, rows, cols, col):
            dvl = None
            pieces = [(r, c0) for r in range(rows // 128)
                      for c0 in range(0, cols, 4096)]
            for pi, (r, c0) in enumerate(pieces):
                wdt = min(4096, cols - c0)
                dd = dma("sync", "bl", big3[:, 0:wdt],
                         slc[r * 128:(r + 1) * 128, c0:c0 + wdt],
                         waits=[dvl, rd(big3)])
                tgt = st[:, col:col + 1] if pi == 0 else st[:, col + 4:col + 5]
                dvl = comp("vector",
                           lambda E, tg=tgt, wd=wdt: E.reduce_sum(
                               tg, big3[:, 0:wd], AX, apply_absolute_value=True),
                           waits=[dd])
                setrd(big3, dvl)
                if pi > 0:
                    dvl = comp("vector", lambda E, c=col: E.tensor_add(
                        st[:, c:c + 1], st[:, c:c + 1], st[:, c + 4:c + 5]),
                        waits=[dvl])
            return dvl

        d1 = p0_sum(w1s, 256, DFF, 0)
        d2 = p0_sum(w2s, 256, DFF, 1)
        d3 = p0_sum(w3s, 1024, C, 2)
        rdep = None
        for j, dep in enumerate((d1, d2, d3)):
            f1 = dma("scalar", "p0f", pflip, st[:, j:j + 1], waits=[dep, rdep])
            f2 = dma("scalar", "p0f", srow[0:1, 0:128],
                     pflip.rearrange("p one -> one p"), waits=[f1])
            rdep = comp("vector", lambda E, j=j: E.reduce_sum(
                srow[0:1, 128 + j:129 + j], srow[0:1, 0:128], AX), waits=[f2])
        dz = comp("vector", lambda E: E.memset(srow[0:1, 131:132], 0.0))
        da = dma("scalar", "p0f", ar_in, srow[0:1, 128:132], waits=[rdep, dz])
        dar = coll(lambda E: E.collective_compute(
            "AllReduce", ALU.add, replica_groups=[list(range(NCORES))],
            ins=[ar_in.opt()], outs=[ar_out.opt()]), waits=[da])
        dld = dma("scalar", "p0f", srow[0:1, 136:140], ar_out, waits=[dar])
        dmw = comp("vector", lambda E: E.tensor_scalar(
            srow[0:1, 132:135], srow[0:1, 136:139], 1.0 / NELEM_W / 127.0, None,
            ALU.mult), waits=[dld])
        dmm_ = comp("vector", lambda E: E.tensor_scalar(
            srow[0:1, 136:139], srow[0:1, 136:139], 1.0 / NELEM_W, None,
            ALU.mult), waits=[dmw])
        div = comp("vector", lambda E: E.reciprocal(
            srow[0:1, 140:143], srow[0:1, 136:139]), waits=[dmm_])
        dg1 = comp("vector", lambda E: E.tensor_copy(
            srow[0:1, 0:3], srow[0:1, 132:135]), waits=[div])
        dg2 = comp("vector", lambda E: E.tensor_copy(
            srow[0:1, 3:6], srow[0:1, 140:143]), waits=[dg1])
        dbc = comp("tensor", lambda E: E.matmul(
            pb[5][:, 0:6], ones1[0:1, :], srow[0:1, 0:6], start=True, stop=True),
            waits=[dg2, dep_consts, bank_free[5]])
        bank_fill[5] = dbc
        dbc2 = comp("vector", lambda E: E.tensor_copy(bc[:, 0:6], pb[5][:, 0:6]),
                    waits=[dbc])
        bank_free[5] = dbc2

        def p0_tern(slc, rows, cols, ivcol, dst):
            dprev = None
            pieces = [(r, c0) for r in range(rows // 128)
                      for c0 in range(0, cols, 4096)]
            for r, c0 in pieces:
                wdt = min(4096, cols - c0)
                dd = dma("sync", "bl", big3[:, 0:wdt],
                         slc[r * 128:(r + 1) * 128, c0:c0 + wdt],
                         waits=[dbc2, dprev, rd(big3)])
                dv = comp("vector", lambda E, wd=wdt, c=ivcol: E.tensor_scalar(
                    big3[:, 0:wd], big3[:, 0:wd], bc[:, c:c + 1], MAGIC,
                    ALU.mult, ALU.add), waits=[dd])
                dv = comp("vector", lambda E, wd=wdt: E.tensor_scalar(
                    big3[:, 0:wd], big3[:, 0:wd], MAGIC, 1.0,
                    ALU.subtract, ALU.min), waits=[dv])
                dv = comp("vector", lambda E, wd=wdt: E.tensor_scalar(
                    wbuf[:, 0:wd], big3[:, 0:wd], -1.0, None, ALU.max),
                    waits=[dv, rd(wbuf)])
                setrd(big3, dv)
                dprev = dma("scalar", "agw", dst[r * 128:(r + 1) * 128, c0:c0 + wdt],
                            wbuf[:, 0:wdt], waits=[dv])
                setrd(wbuf, dprev)
            return dprev

        dt1 = p0_tern(w1s, 256, DFF, 3, ag1_in[0:256, :])
        dt2 = p0_tern(w2s, 256, DFF, 4, ag1_in[256:512, :])
        dt3 = p0_tern(w3s, 1024, C, 5, ag2_in)
        dag1 = coll(lambda E: E.collective_compute(
            "AllGather", ALU.bypass, replica_groups=[list(range(NCORES))],
            ins=[ag1_in.opt()], outs=[ag1_out.opt()]), waits=[dt1, dt2])
        dag2 = coll(lambda E: E.collective_compute(
            "AllGather", ALU.bypass, replica_groups=[list(range(NCORES))],
            ins=[ag2_in.opt()], outs=[ag2_out.opt()]), waits=[dt3])

        xt_free = [None, None]
        phase_entry = [dep_xcopy]

        def load_xtile(t, src=x_dram):
            slot = t % 2
            d = dma("sync", f"xl{slot}", xt[slot][:], src[t * 128:(t + 1) * 128, :],
                    waits=[xt_free[slot], phase_entry[0]])
            return d, slot

        def ln_stats(t, slot, dxt, want_minmax):
            b = t * 12
            dsum = comp("vector", lambda E, b=b, slot=slot: E.reduce_sum(
                st[:, b:b + 1], xt[slot][:], AX), waits=[dxt])
            dsq = comp("scalar", lambda E, b=b, slot=slot: E.activation(
                sqs[:], xt[slot][:], AF.Square, accum_out=st[:, b + 1:b + 2]),
                waits=[dxt, rd(sqs)])
            setrd(sqs, dsq)
            deps = [dsum, dsq]
            if want_minmax:
                deps.append(comp("vector", lambda E, b=b, slot=slot:
                                 E.tensor_reduce(
                    st[:, b + 2:b + 3], xt[slot][:], AX, ALU.max), waits=[dxt]))
                deps.append(comp("vector", lambda E, b=b, slot=slot:
                                 E.tensor_reduce(
                    st[:, b + 3:b + 4], xt[slot][:], AX, ALU.min), waits=[dxt]))
            return deps

        def transpose16(src_bf, dsrc, dst_of, nt_=16, extra_waits=()):
            dv = None
            for h0 in range(0, nt_, 8):
                n = min(8, nt_ - h0)
                half = (h0 // 8) % 2
                pt = ptr2[half]
                dt = None
                for j in range(n):
                    dt = comp("tensor", lambda E, j=j, jj=h0 + j, pt=pt:
                              E.transpose(
                        pt[:, j * 128:(j + 1) * 128],
                        src_bf[:, jj * 128:(jj + 1) * 128], idn[:]),
                        waits=[dsrc, ptr_free[half], dep_consts] if j == 0
                        else [dsrc])
                    ptr_fill[half] = dt
                dv = comp("vector", lambda E, h0=h0, n=n, pt=pt:
                          E.tensor_copy(
                    dst_of(h0, n),
                    pt[:, 0:n * 128]
                    .rearrange("p (nn z) -> p nn z", z=128)),
                    waits=[dt] + list(extra_waits))
                ptr_free[half] = dv
            return dv

        def fm_dst(buf, stride, base, toff):
            def f(j0, n, buf=buf, stride=stride, base=base, toff=toff):
                v = buf[:].rearrange("p (ct z) -> p ct z", z=stride)
                return v[:, base + j0: base + j0 + n, toff:toff + 128]
            return f

        def ffn_phase(dst, wsem):
            dcoefs = [None] * NT
            for t in range(NT):
                dxt, slot = load_xtile(t)
                deps = ln_stats(t, slot, dxt, True)
                b = t * 12
                cS = lambda i, b=b: st[:, b + i:b + i + 1]
                d = comp("vector", lambda E, cS=cS: E.tensor_scalar(
                    cS(9), cS(0), 1.0 / C, None, ALU.mult), waits=deps)
                d = comp("vector", lambda E, cS=cS: E.tensor_scalar(
                    cS(10), cS(1), 1.0 / C, None, ALU.mult), waits=[d])
                d = comp("vector", lambda E, cS=cS: E.scalar_tensor_tensor(
                    cS(11), cS(9), 1.0, cS(9), ALU.mult, ALU.mult), waits=[d])
                d = comp("vector", lambda E, cS=cS: E.tensor_sub(
                    cS(10), cS(10), cS(11)), waits=[d])
                d = comp("vector", lambda E, cS=cS: E.tensor_scalar(
                    cS(11), cS(10), 1.000001, 1e-11, ALU.mult, ALU.add), waits=[d])
                d = comp("scalar", lambda E, cS=cS: E.activation(
                    cS(11), cS(11), AF.Sqrt), waits=[d])
                d = comp("vector", lambda E, cS=cS: E.reciprocal(cS(11), cS(11)),
                         waits=[d])
                d2 = comp("vector", lambda E, cS=cS: E.tensor_sub(
                    cS(2), cS(2), cS(9)), waits=[d])
                d2 = comp("vector", lambda E, cS=cS: E.tensor_sub(
                    cS(3), cS(9), cS(3)), waits=[d2])
                d2 = comp("vector", lambda E, cS=cS: E.tensor_tensor(
                    cS(10), cS(2), cS(3), ALU.max), waits=[d2])
                d2 = comp("vector", lambda E, cS=cS: E.tensor_mul(
                    cS(8), cS(10), cS(11)), waits=[d2])
                d2 = comp("vector", lambda E, cS=cS: E.tensor_scalar(
                    cS(8), cS(8), 1e-5, None, ALU.max), waits=[d2])
                d2 = comp("vector", lambda E, cS=cS: E.reciprocal(
                    cS(2), cS(8)), waits=[d2])
                d2 = comp("vector", lambda E, cS=cS: E.tensor_mul(
                    cS(4), cS(11), cS(2)), waits=[d2])
                d2 = comp("vector", lambda E, cS=cS: E.tensor_scalar(
                    cS(4), cS(4), 127.0, None, ALU.mult), waits=[d2])
                d2 = comp("vector", lambda E, cS=cS: E.scalar_tensor_tensor(
                    cS(5), cS(9), -1.0, cS(4), ALU.mult, ALU.mult), waits=[d2])
                d2 = comp("vector", lambda E, cS=cS: E.tensor_mul(
                    cS(6), cS(8), bc[:, 0:1]), waits=[d2, dbc2])
                d2 = comp("vector", lambda E, cS=cS: E.tensor_mul(
                    cS(7), cS(8), bc[:, 1:2]), waits=[d2])
                du = comp("scalar", lambda E, slot=slot, cS=cS: E.activation(
                    sqs[:], xt[slot][:], AF.Identity, bias=cS(5), scale=cS(4)),
                    waits=[d2, rd(sqs)])
                du = comp("vector", lambda E: E.tensor_scalar(
                    sqs[:], sqs[:], MAGIC, MAGIC, ALU.add, ALU.subtract),
                    waits=[du])
                du = comp("vector", lambda E: E.tensor_scalar(
                    aux[:], sqs[:], 127.0, -128.0, ALU.min, ALU.max),
                    waits=[du, rd(aux)])
                setrd(sqs, du)
                dq = transpose16(aux, du, fm_dst(fm, 1024, 0, t * 128),
                                 extra_waits=[rd(fm)])
                setrd(aux, dq)
                xt_free[slot] = du
                dm0 = comp("vector", lambda E, cS=cS: E.memset(cS(2), 0.0),
                           waits=[d2])
                dm1 = comp("vector", lambda E, cS=cS: E.memset(cS(3), 0.0),
                           waits=[d2])
                dcoefs[t] = (d2, dm0, dm1)
                afm_all = dq

            wslot_free = [None, None]
            hsa_free = [None] * NT
            hslot_free = hslot_free_attn
            last_mm12 = None
            for o in range(DFF // 256):
                for w12 in range(2):
                    ci = 2 * o + w12
                    sl = ci % 2
                    rows = w1t_rows if w12 == 0 else w2t_rows
                    dwl = None
                    for kk in range(0, 16, 2):
                        dwl = dma(
                            "sync", f"wl{sl}",
                            wbuf[:, sl * 4096 + kk * 256:sl * 4096 + (kk + 2) * 256]
                            .rearrange("p (n z) -> p n z", n=2),
                            ag1_out[rows(kk):rows(kk) + 256, o * 256:(o + 1) * 256]
                            .rearrange("(n p) z -> p n z", p=128),
                            waits=[wslot_free[sl], dag1, rd(wbuf)]
                            if kk == 0 else [])
                    dmm = None
                    for t in range(NT):
                        bank = t % 4
                        b = t * 12
                        for k in range(16):
                            dmm = comp("tensor", lambda E, k=k, t=t, bank=bank,
                                       sl=sl: E.matmul(
                                pb[bank][:, 0:256],
                                fm[:, k * 1024 + t * 128:k * 1024 + (t + 1) * 128],
                                wbuf[:, sl * 4096 + k * 256:sl * 4096 + (k + 1) * 256],
                                start=(k == 0), stop=(k == 15)),
                                waits=[dwl, bank_free[bank], afm_all,
                                       dcoefs[t][0]] if k == 0 else [])
                        bank_fill[bank] = dmm
                        last_mm12 = dmm
                        if w12 == 0:
                            dsil = comp("scalar", lambda E, bank=bank, b=b, t=t:
                                        E.activation(
                                qst[:, t * 256:(t + 1) * 256],
                                pb[bank][:, 0:256], AF.Silu,
                                scale=st[:, b + 6:b + 7]),
                                waits=[bank_fill[bank], hsa_free[t]])
                            bank_free[bank] = dsil
                            hsa_free[t] = dsil
                        else:
                            hslot = (o * 8 + t) % 8
                            dh = comp("vector", lambda E, bank=bank, b=b, t=t,
                                      hslot=hslot:
                                      E.scalar_tensor_tensor(
                                hstg[:, hslot * 256:hslot * 256 + 256],
                                pb[bank][:, 0:256], st[:, b + 7:b + 8],
                                qst[:, t * 256:(t + 1) * 256],
                                ALU.mult, ALU.mult),
                                waits=[bank_fill[bank], hsa_free[t],
                                       hslot_free[hslot]])
                            bank_free[bank] = dh
                            hsa_free[t] = dh
                            dv = comp("vector", lambda E, hslot=hslot:
                                      E.reduce_max(
                                sv[:, 0:1],
                                hstg[:, hslot * 256:hslot * 256 + 256],
                                AX, apply_absolute_value=True), waits=[dh])
                            dv = comp("vector", lambda E, b=b: E.tensor_tensor(
                                st[:, b + 2:b + 3], st[:, b + 2:b + 3],
                                sv[:, 0:1], ALU.max), waits=[dv, dcoefs[t][1]])
                            dg = comp("scalar", lambda E, hslot=hslot,
                                      t=t: E.activation(
                                aux[:, 0:256],
                                hstg[:, hslot * 256:hslot * 256 + 256],
                                AF.Square,
                                accum_out=sv[:, 4 + (t % 4):5 + (t % 4)]),
                                waits=[dh, rd(aux)])
                            setrd(aux, dg)
                            dg = comp("vector", lambda E, b=b, t=t: E.tensor_add(
                                st[:, b + 3:b + 4], st[:, b + 3:b + 4],
                                sv[:, 4 + (t % 4):5 + (t % 4)]),
                                waits=[dg, dcoefs[t][2]])
                            dsp = dma("scalar", f"hs{hslot}",
                                      h_dram[t * 128:(t + 1) * 128,
                                             o * 256:(o + 1) * 256],
                                      hstg[:, hslot * 256:hslot * 256 + 256],
                                      waits=[dh, dv, dg])
                            hslot_free[hslot] = dsp
                    wslot_free[sl] = dmm
            setrd(fm, last_mm12)
            setrd(wbuf, last_mm12)
            hdeps = [None] * NT
            for t in range(NT):
                b = t * 12
                cS = lambda i, b=b: st[:, b + i:b + i + 1]
                last_stat = list(hslot_free) + [bank_free[i] for i in range(4)]
                d = comp("vector", lambda E, cS=cS: E.tensor_scalar(
                    cS(10), cS(3), 1.0 / DFF, 1e-6, ALU.mult, ALU.add),
                    waits=last_stat)
                d = comp("scalar", lambda E, cS=cS: E.activation(
                    cS(10), cS(10), AF.Sqrt), waits=[d])
                d = comp("vector", lambda E, cS=cS: E.reciprocal(
                    cS(10), cS(10)), waits=[d])
                d = comp("vector", lambda E, cS=cS: E.tensor_mul(
                    cS(8), cS(10), cS(2)), waits=[d])
                d = comp("vector", lambda E, cS=cS: E.tensor_scalar(
                    cS(8), cS(8), 1e-5, None, ALU.max), waits=[d])
                d = comp("vector", lambda E, cS=cS: E.reciprocal(
                    cS(9), cS(8)), waits=[d])
                d = comp("vector", lambda E, cS=cS: E.tensor_mul(
                    cS(4), cS(10), cS(9)), waits=[d])
                d = comp("vector", lambda E, cS=cS: E.tensor_scalar(
                    cS(4), cS(4), 127.0, None, ALU.mult), waits=[d])
                d = comp("vector", lambda E, cS=cS: E.scalar_tensor_tensor(
                    cS(6), cS(8), 0.5, bc[:, 2:3], ALU.mult, ALU.mult),
                    waits=[d])
                hdeps[t] = d
            w3sl = [None] * 6
            dwb = None
            nb = 0
            for g in range(NT // 4):
                ts4 = [4 * g + j for j in range(4)]
                dq3s = [None] * 4
                for ti, t in enumerate(ts4):
                    b = t * 12
                    dq3 = None
                    for hv in range(2):
                        drl = dma("scalar", "hrl", big3[:],
                                  h_dram[t * 128:(t + 1) * 128,
                                         hv * 4096:(hv + 1) * 4096],
                                  waits=list(hslot_free) + [rd(big3)])
                        for pch in range(2):
                            pc = hv * 2 + pch
                            du = comp("scalar", lambda E, pch=pch, b=b:
                                      E.activation(
                                sqs[:], big3[:, pch * 2048:(pch + 1) * 2048],
                                AF.Copy, bias=MAGIC, scale=st[:, b + 4:b + 5]),
                                waits=[drl, hdeps[t], rd(sqs)])
                            du = comp("vector", lambda E: E.tensor_scalar(
                                sqs[:], sqs[:], MAGIC, 127.0,
                                ALU.subtract, ALU.min), waits=[du])
                            du = comp("vector", lambda E: E.tensor_scalar(
                                aux[:], sqs[:], -128.0, None, ALU.max),
                                waits=[du, rd(aux)])
                            setrd(sqs, du)
                            dq3 = transpose16(
                                aux, du, fm_dst(fm, 512, pc * 16, ti * 128),
                                extra_waits=[rd(fm)])
                            setrd(aux, dq3)
                        setrd(big3, dq3)
                    dq3s[ti] = dq3
                dx0, s0 = load_xtile(ts4[0])
                dx1, s1 = load_xtile(ts4[1])
                dx2 = dma("sync", "xl2", qraw[:],
                          x_dram[ts4[2] * 128:(ts4[2] + 1) * 128, :],
                          waits=[rd(qraw), phase_entry[0]])
                dx3 = dma("sync", "xl3", hstg[:],
                          x_dram[ts4[3] * 128:(ts4[3] + 1) * 128, :],
                          waits=[rd(hstg), phase_entry[0]] + list(hslot_free))
                xts4 = [xt[s0], xt[s1], qraw, hstg]
                dxs4 = [dx0, dx1, dx2, dx3]
                evdep = [[None] * 4 for _ in range(4)]
                dlast_fm = None
                for p in range(4):
                    banks = [(nb + j) % 6 for j in range(4)]
                    nb += 4
                    dlast = None
                    for i in range(64):
                        li = p * 64 + i
                        ssl = li % 6
                        dwd = dma("sync", f"w3a{ssl}",
                                  w3b[:, ssl * 512:(ssl + 1) * 512],
                                  ag2_out[w3t_rows(i):w3t_rows(i) + 128,
                                          p * 512:(p + 1) * 512],
                                  waits=[w3sl[ssl], dag2])
                        for ti in range(4):
                            bk = banks[ti]
                            dlast = comp("tensor", lambda E, i=i, ti=ti, bk=bk,
                                         ssl=ssl: E.matmul(
                                pb[bk][:],
                                fm[:, i * 512 + ti * 128:i * 512 + (ti + 1) * 128],
                                w3b[:, ssl * 512:(ssl + 1) * 512],
                                start=(i == 0), stop=(i == 63)),
                                waits=([dwd] + dq3s + [bank_free[b_]
                                                       for b_ in banks])
                                if (i == 0 and ti == 0) else
                                ([dwd] if ti == 0 else []))
                        w3sl[ssl] = dlast
                    dlast_fm = dlast
                    for ti in range(4):
                        bk = banks[ti]
                        bank_fill[bk] = dlast
                        b = ts4[ti] * 12
                        dv = comp("vector", lambda E, bk=bk, ti=ti, p=p, b=b:
                                  E.scalar_tensor_tensor(
                            xts4[ti][:, p * 512:(p + 1) * 512], pb[bk][:],
                            st[:, b + 6:b + 7],
                            xts4[ti][:, p * 512:(p + 1) * 512],
                            ALU.mult, ALU.add),
                            waits=[bank_fill[bk], dxs4[ti]])
                        bank_free[bk] = dv
                        evdep[ti][p] = dv
                for ti in range(4):
                    t = ts4[ti]
                    dwb = dma("scalar", wsem, dst[t * 128:(t + 1) * 128, :],
                              xts4[ti][:], waits=evdep[ti] + [dwb])
                    if ti == 0:
                        xt_free[s0] = dwb
                    elif ti == 1:
                        xt_free[s1] = dwb
                setrd(qraw, dwb)
                setrd(hstg, dwb)
                setrd(fm, dlast_fm)
            return

        hslot_free_attn = [None] * 8
        ATT = {"gate_w": [None, None], "proj_w": [None, None],
               "qkv_done": None, "q_done": None}

        def attn_phase():
            dq = None
            for t in range(NT):
                dxt, slot = load_xtile(t)
                deps = ln_stats(t, slot, dxt, False)
                b = t * 12
                cS = lambda i, b=b: st[:, b + i:b + i + 1]
                d = comp("vector", lambda E, cS=cS: E.tensor_scalar(
                    cS(9), cS(0), 1.0 / C, None, ALU.mult), waits=deps)
                d = comp("vector", lambda E, cS=cS: E.tensor_scalar(
                    cS(10), cS(1), 1.0 / C, None, ALU.mult), waits=[d])
                d = comp("vector", lambda E, cS=cS: E.scalar_tensor_tensor(
                    cS(11), cS(9), 1.0, cS(9), ALU.mult, ALU.mult), waits=[d])
                d = comp("vector", lambda E, cS=cS: E.tensor_sub(
                    cS(10), cS(10), cS(11)), waits=[d])
                d = comp("vector", lambda E, cS=cS: E.tensor_scalar(
                    cS(10), cS(10), 1.0, 1e-5, ALU.mult, ALU.add), waits=[d])
                d = comp("scalar", lambda E, cS=cS: E.activation(
                    cS(10), cS(10), AF.Sqrt), waits=[d])
                d = comp("vector", lambda E, cS=cS: E.reciprocal(
                    cS(10), cS(10)), waits=[d])
                d = comp("vector", lambda E, cS=cS: E.scalar_tensor_tensor(
                    cS(11), cS(9), -1.0, cS(10), ALU.mult, ALU.mult), waits=[d])
                dxl = comp("scalar", lambda E, slot=slot, cS=cS: E.activation(
                    aux[:], xt[slot][:], AF.Identity, bias=cS(11), scale=cS(10)),
                    waits=[d, rd(aux)])
                dq = transpose16(aux, dxl, fm_dst(fm, 1024, 0, t * 128),
                                 extra_waits=[rd(fm)])
                setrd(aux, dq)
                xt_free[slot] = dxl

            wslot_free = [None, None]
            kin_done = []
            vin_done = []
            fc_list = [("k", i) for i in range(8)] + [("v", i) for i in range(8)] \
                + [("q", i) for i in range(8)]
            dmmlast = None
            dagkv = [None, None]
            for kind, i in fc_list:
                fc = {"q": 0, "k": 8, "v": 16}[kind] + i
                sl = fc % 2
                fcol = {"q": 0, "k": C, "v": 2 * C}[kind] + i * 256
                dL0 = dma("sync", "qw0",
                          big3[:, 0:2048].rearrange("p (n f) -> p n f", n=8),
                          wqkv[0:1024, fcol:fcol + 256]
                          .rearrange("(n p) f -> p n f", p=128),
                          waits=[rd(big3)])
                dL1 = dma("scalar", "qw1",
                          big3[:, 2048:4096].rearrange("p (n f) -> p n f", n=8),
                          wqkv[1024:2048, fcol:fcol + 256]
                          .rearrange("(n p) f -> p n f", p=128),
                          waits=[rd(big3)])
                dc0 = comp("scalar", lambda E, sl=sl: E.activation(
                    wbuf[:, sl * 4096:sl * 4096 + 2048], big3[:, 0:2048],
                    AF.Copy),
                    waits=[dL0, wslot_free[sl], rd(wbuf)])
                dwd = comp("scalar", lambda E, sl=sl: E.activation(
                    wbuf[:, sl * 4096 + 2048:(sl + 1) * 4096],
                    big3[:, 2048:4096], AF.Copy), waits=[dL1])
                setrd(big3, dwd)
                if kind != "v":
                    lcol = i * 256
                    dcs = dma("sync", "csm0",
                              sqs[:].rearrange("p (t f) -> p t f", f=256),
                              cosb[:, lcol:lcol + 256]
                              .rearrange("(t p) f -> p t f", p=128),
                              waits=[rd(sqs)])
                    dsn = dma("scalar", "csm1",
                              hstg[:].rearrange("p (t f) -> p t f", f=256),
                              sinb[:, lcol:lcol + 256]
                              .rearrange("(t p) f -> p t f", p=128),
                              waits=[rd(hstg)])
                for t in range(NT):
                    bank = t % 3
                    dmm = None
                    for k in range(16):
                        dmm = comp("tensor", lambda E, k=k, t=t, bank=bank,
                                   sl=sl: E.matmul(
                            pb[bank][:, 0:256],
                            fm[:, k * 1024 + t * 128:k * 1024 + (t + 1) * 128],
                            wbuf[:, sl * 4096 + k * 256:sl * 4096 + (k + 1) * 256],
                            start=(k == 0), stop=(k == 15)),
                            waits=[dwd, bank_free[bank], dq] if k == 0 else [])
                    bank_fill[bank] = dmm
                    dmmlast = dmm
                    if kind == "v":
                        dvc = comp("vector", lambda E, bank=bank: E.tensor_copy(
                            smb[:, 0:256], pb[bank][:, 0:256]),
                            waits=[bank_fill[bank], rd(smb)])
                        dst_v = v_in.rearrange("(h tt) dd -> tt h dd", h=H)[
                            t * 128:(t + 1) * 128, i * 2:(i + 1) * 2, :]
                        dvw = dma("scalar", "kvw", dst_v,
                                  smb[:, 0:256]
                                  .rearrange("p (n dd) -> p n dd", n=2),
                                  waits=[dvc])
                        setrd(smb, dvw)
                        vin_done.append(dvw)
                        bank_free[bank] = dvc
                        continue
                    dcp = comp("scalar", lambda E, bank=bank, t=t: E.activation(
                        qraw[:, t * 256:(t + 1) * 256], pb[bank][:, 0:256],
                        AF.Copy), waits=[bank_fill[bank], rd(qraw)])
                    bank_free[bank] = dcp
                    d1 = comp("vector", lambda E, t=t: E.tensor_mul(
                        qst[:, 0:256], qraw[:, t * 256:(t + 1) * 256],
                        sqs[:, t * 256:(t + 1) * 256]),
                        waits=[dcs, dcp, rd(qst)])
                    ybv = qraw[:, t * 256:(t + 1) * 256].rearrange(
                        "p (n two) -> p n two", two=2)
                    d2 = comp("vector", lambda E, ybv=ybv, t=t: E.tensor_mul(
                        qst[:, 256:512].rearrange("p (n two) -> p n two", two=2),
                        ybv[:, :, ::-1],
                        hstg[:, t * 256:(t + 1) * 256]
                        .rearrange("p (n two) -> p n two", two=2)),
                        waits=[dsn, d1])
                    d3 = comp("vector", lambda E: E.tensor_add(
                        qst[:, 0:256], qst[:, 0:256], qst[:, 256:512]),
                        waits=[d2])
                    setrd(qst, d3)
                    setrd(qraw, d2)
                    setrd(sqs, d1)
                    setrd(hstg, d2)
                    for j in range(2):
                        hh = i * 2 + j
                        scol = (qde[:, hh * 4 + (t % 4):hh * 4 + (t % 4) + 1]
                                if kind == "q" else kde[:, hh:hh + 1])
                        d3 = comp("vector", lambda E, j=j, scol=scol:
                                  E.tensor_scalar(
                            aux[:, j * 128:(j + 1) * 128],
                            qst[:, j * 128:(j + 1) * 128], scol, None, ALU.mult),
                            waits=[d3, rd(aux), dep_consts] if j == 0 else [d3])
                    if kind == "q":
                        dtp = transpose16(
                            aux, d3, fm_dst(fm, 1024, 16 + i * 2, t * 128),
                            nt_=2, extra_waits=[rd(fm)])
                        setrd(aux, dtp)
                        ATT["q_done"] = dtp
                    else:
                        dtp = transpose16(
                            aux, d3,
                            lambda j0, n: smb[:, 256 + j0 * 128:
                                              256 + (j0 + n) * 128]
                            .rearrange("p (nn z) -> p nn z", z=128),
                            nt_=2, extra_waits=[rd(smb)])
                        setrd(aux, dtp)
                        dst_k = k_in.rearrange("(h dd) tt -> dd h tt", h=H)[
                            :, i * 2:(i + 1) * 2, t * 128:(t + 1) * 128]
                        dkw = dma("scalar", "kvw", dst_k,
                                  smb[:, 256:512]
                                  .rearrange("p (n tt) -> p n tt", n=2),
                                  waits=[dtp])
                        setrd(smb, dkw)
                        kin_done.append(dkw)
                wslot_free[sl] = dmmlast
                if i == 7 and kind == "k":
                    dagkv[0] = coll(lambda E: E.collective_compute(
                        "AllGather", ALU.bypass,
                        replica_groups=[[0, 1, 2, 3], [4, 5, 6, 7]],
                        ins=[k_in.opt()], outs=[k_out.opt()]),
                        waits=[kin_done[-1]])
                elif i == 7 and kind == "v":
                    dagkv[1] = coll(lambda E: E.collective_compute(
                        "AllGather", ALU.bypass,
                        replica_groups=[[0, 1, 2, 3], [4, 5, 6, 7]],
                        ins=[v_in.opt()], outs=[v_out.opt()]),
                        waits=[vin_done[-1]])
            ATT["qkv_done"] = dmmlast
            ATT["qkv_vdone"] = (vS, sch.n(vS))
            dagk, dagv = dagkv
            dep_kin = kin_done[-1]
            dep_vin = vin_done[-1]
            dzlast_h = [None]

            def emit_stats_gn(pd):
                h, ypb, fill = pd["h"], pd["ypb"], pd["fill"]
                for tt in range(4):
                    i4 = h * 4 + tt
                    comp("vector", lambda E, i4=i4, tt=tt, ypb=ypb:
                         E.reduce_sum(st[:, 96 + i4:97 + i4],
                                      pb[ypb][:, tt * 128:(tt + 1) * 128],
                                      AX), waits=[fill])
                vred = (vS, sch.n(vS))
                dstat = None
                for tt in range(4):
                    i4 = h * 4 + tt
                    dstat = comp("scalar", lambda E, i4=i4, tt=tt, ypb=ypb:
                                 E.activation(
                        sqs[:, 0:128], pb[ypb][:, tt * 128:(tt + 1) * 128],
                        AF.Square, accum_out=st[:, 160 + i4:161 + i4]),
                        waits=[fill, rd(sqs), vred])
                    setrd(sqs, dstat)
                c0 = 96 + 4 * h
                d = comp("vector", lambda E, c0=c0: E.tensor_scalar(
                    st[:, c0 + 128:c0 + 132], st[:, c0:c0 + 4],
                    -1.0 / D, None, ALU.mult), waits=[vred])
                d = comp("vector", lambda E, c0=c0: E.tensor_scalar(
                    st[:, c0 + 64:c0 + 68], st[:, c0 + 64:c0 + 68],
                    1.0 / D, None, ALU.mult), waits=[d, dstat])
                d = comp("vector", lambda E, c0=c0: E.scalar_tensor_tensor(
                    st[:, c0 + 192:c0 + 196], st[:, c0 + 128:c0 + 132], 1.0,
                    st[:, c0 + 128:c0 + 132], ALU.mult, ALU.mult), waits=[d])
                d = comp("vector", lambda E, c0=c0: E.tensor_sub(
                    st[:, c0 + 64:c0 + 68], st[:, c0 + 64:c0 + 68],
                    st[:, c0 + 192:c0 + 196]), waits=[d])
                d = comp("vector", lambda E, c0=c0: E.tensor_scalar(
                    st[:, c0 + 64:c0 + 68], st[:, c0 + 64:c0 + 68],
                    1.0, 1e-5, ALU.mult, ALU.add), waits=[d])
                d = comp("scalar", lambda E, c0=c0: E.activation(
                    st[:, c0 + 192:c0 + 196], st[:, c0 + 64:c0 + 68],
                    AF.Sqrt), waits=[d])
                pd["drs"] = comp("vector", lambda E, c0=c0: E.reciprocal(
                    st[:, c0 + 192:c0 + 196], st[:, c0 + 192:c0 + 196]),
                    waits=[d])

            def emit_ztt(pd, tt):
                h, ypb = pd["h"], pd["ypb"]
                i4 = h * 4 + tt
                gsl = g32sl(tt, h)
                zsl = gch[:, tt * 2048 + h * 128:tt * 2048 + (h + 1) * 128]
                dza = comp("vector", lambda E, tt=tt, ypb=ypb, i4=i4:
                           E.tensor_scalar(
                    qst[:, 0:128], pb[ypb][:, tt * 128:(tt + 1) * 128],
                    st[:, 224 + i4:225 + i4], st[:, 288 + i4:289 + i4],
                    ALU.add, ALU.mult),
                    waits=[pd["drs"], pd["gate"], rd(qst)])
                d1 = comp("vector", lambda E, gsl=gsl: E.tensor_mul(
                    qst[:, 128:256], qst[:, 0:128], gsl),
                    waits=[dza])
                dt1 = comp("tensor", lambda E: E.transpose(
                    pb[5][:, 0:128], qst[:, 128:256], idnf[:]),
                    waits=[d1, bank_free[5], dep_consts])
                dt2 = comp("tensor", lambda E, gsl=gsl: E.transpose(
                    pb[5][:, 128:256], gsl, idnf[:]), waits=[dt1])
                setrd(qst, dt1)
                setrd(g32buf(tt, h), dt2)
                bank_fill[5] = dt2
                ev1 = comp("vector", lambda E, h=h: E.tensor_scalar(
                    sqs[:, 0:128], pb[5][:, 128:256],
                    gnb[:, h:h + 1], None, ALU.mult),
                    waits=[dt2, rd(sqs)])
                ev2 = comp("vector", lambda E, h=h, zsl=zsl:
                           E.scalar_tensor_tensor(
                    zsl, pb[5][:, 0:128], gnw[:, h:h + 1],
                    sqs[:, 0:128], ALU.mult, ALU.add),
                    waits=[ev1, rd(gch)])
                setrd(sqs, ev2)
                bank_free[5] = ev2
                dzlast_h[0] = ev2
                pd["dza"] = dza

            pend = [None]

            def drain_step():
                pd = pend[0]
                if pd is None:
                    return
                if pd["stage"] == 0:
                    emit_stats_gn(pd)
                else:
                    emit_ztt(pd, pd["stage"] - 1)
                pd["stage"] += 1
                if pd["stage"] == 5:
                    bank_free[pd["ypb"]] = pd["dza"]
                    pend[0] = None

            kslot_free = [None, None, None]
            for chunk in range(2):
                dgate = emit_gate(chunk)
                plist = [p for p, pl in enumerate(PASS_LIST) if pl[0] == chunk]
                np_ = len(plist)
                for h in range(H):
                    ypb = 3 + (h % 2)
                    nacc = 0
                    dav = None
                    for pi, p in enumerate(plist):
                        _, src = PASS_LIST[p]
                        ksl = pi % 3
                        if src[0] == "diag":
                            ksrc = k_in[h * 128:(h + 1) * 128,
                                        chunk * 512:(chunk + 1) * 512]
                            vsrc = v_in[h * TOK + chunk * 512:
                                        h * TOK + (chunk + 1) * 512, :]
                            kw = [kslot_free[ksl], dep_kin]
                            vw = [kslot_free[ksl], dep_vin]
                        else:
                            slot, sub = src[1], src[2]
                            ksrc = k_out[slot * 2048 + h * 128:
                                         slot * 2048 + (h + 1) * 128,
                                         sub * 512:(sub + 1) * 512]
                            vsrc = v_out[slot * (H * TOK) + h * TOK + sub * 512:
                                         slot * (H * TOK) + h * TOK
                                         + (sub + 1) * 512, :]
                            kw = [kslot_free[ksl], dagk]
                            vw = [kslot_free[ksl], dagv]
                        dk = dma("sync", f"klm{ksl}", kst[:, ksl * 512:(ksl + 1) * 512],
                                 ksrc, waits=kw)
                        dvv = dma("sync", f"vlm{ksl}",
                                  vst[:, ksl * 512:(ksl + 1) * 512]
                                  .rearrange("p (n dd) -> p n dd", n=4),
                                  vsrc.rearrange("(n p) dd -> p n dd", p=128),
                                  waits=vw)
                        for q in range(4):
                            sbank = q % 3
                            dsc = comp("tensor", lambda E, q=q, ksl=ksl, h=h,
                                       chunk=chunk, sbank=sbank: E.matmul(
                                pb[sbank][:],
                                kst[:, ksl * 512 + q * 128:
                                    ksl * 512 + (q + 1) * 128],
                                fm[:, 16384 + h * 1024 + chunk * 512:
                                   16384 + h * 1024 + (chunk + 1) * 512],
                                start=True, stop=True),
                                waits=[dk, bank_free[sbank], ATT["q_done"]])
                            bank_fill[sbank] = dsc
                            scol = sct[:, (h * NPASS + p) * 4 + q:
                                       (h * NPASS + p) * 4 + q + 1]
                            if src[0] == "diag":
                                dev = comp("vector", lambda E, q=q, sbank=sbank,
                                           scol=scol: E.scalar_tensor_tensor(
                                    smb[:, q * 512:(q + 1) * 512], pb[sbank][:],
                                    scol, tri[:, q * 512:(q + 1) * 512],
                                    ALU.mult, ALU.mult),
                                    waits=[bank_fill[sbank], rd(smb),
                                           dep_consts])
                            else:
                                dev = comp("scalar", lambda E, q=q, sbank=sbank,
                                           scol=scol: E.activation(
                                    smb[:, q * 512:(q + 1) * 512], pb[sbank][:],
                                    AF.Copy, scale=scol),
                                    waits=[bank_fill[sbank], rd(smb)])
                            bank_free[sbank] = dev
                            last_pass = (pi == np_ - 1)
                            for tt in range(4):
                                dav = comp("tensor", lambda E, q=q, tt=tt,
                                           ksl=ksl, ypb=ypb,
                                           first=(nacc == 0 and tt == 0),
                                           stp=(last_pass and q == 3 and tt == 3):
                                           E.matmul(
                                    pb[ypb][:, tt * 128:(tt + 1) * 128],
                                    smb[:, q * 512 + tt * 128:
                                        q * 512 + (tt + 1) * 128],
                                    vst[:, ksl * 512 + q * 128:
                                        ksl * 512 + (q + 1) * 128],
                                    start=first, stop=stp,
                                    skip_group_check=True),
                                    waits=[dev, dvv, bank_free[ypb]]
                                    if tt == 0 else [])
                            setrd(smb, dav)
                            nacc += 1
                        kslot_free[ksl] = dav
                        drain_step()
                    bank_fill[ypb] = dav
                    if chunk == 1 and h == H - 1:
                        setrd(fm, dav)
                    while pend[0] is not None:
                        drain_step()
                    pend[0] = {"h": h, "ypb": ypb, "fill": dav,
                               "gate": dgate, "stage": 0}
                while pend[0] is not None:
                    drain_step()
                attn_tail(chunk, dzlast_h[0])
            for i_ in range(8):
                hslot_free_attn[i_] = rd(hstg)
            return

        def g32buf(tt, h):
            if tt in (0, 1):
                return big3
            if tt == 2:
                return qraw
            return hstg

        def g32sl(tt, h, w=1):
            if tt == 0:
                return big3[:, h * 128:(h + w) * 128]
            if tt == 1:
                return big3[:, 2048 + h * 128:2048 + (h + w) * 128]
            if tt == 2:
                return qraw[:, h * 128:(h + w) * 128]
            return hstg[:, h * 128:(h + w) * 128]

        def emit_gate(chunk):
            wslot_free = ATT["gate_w"]
            dsg = None
            for gc in range(8):
                sl = gc % 2
                dwd = dma("gpsimd", f"gwl{sl}",
                          wbuf[:, sl * 4096:(sl + 1) * 4096]
                          .rearrange("p (n f) -> p n f", n=16),
                          wgd[:, gc * 256:(gc + 1) * 256]
                          .rearrange("(n p) f -> p n f", p=128),
                          waits=[wslot_free[sl], rd(wbuf), ATT["qkv_done"],
                                 rd(gch)])
                for tt in range(4):
                    t = chunk * 4 + tt
                    bank = tt % 3
                    dmm = None
                    for k in range(16):
                        dmm = comp("tensor", lambda E, k=k, t=t, bank=bank,
                                   sl=sl: E.matmul(
                            pb[bank][:, 0:256],
                            fm[:, k * 1024 + t * 128:
                               k * 1024 + (t + 1) * 128],
                            wbuf[:, sl * 4096 + k * 256:
                                 sl * 4096 + (k + 1) * 256],
                            start=(k == 0), stop=(k == 15)),
                            waits=[dwd, bank_free[bank]] if k == 0 else [])
                    bank_fill[bank] = dmm
                    dsg = comp("scalar", lambda E, bank=bank, tt=tt, gc=gc:
                               E.activation(
                        g32sl(tt, gc * 2, 2),
                        pb[bank][:, 0:256], AF.Silu),
                        waits=[bank_fill[bank], rd(big3), rd(qraw),
                               rd(hstg), ATT.get("qkv_vdone")]
                        + list(hslot_free_attn)
                        if gc == 0 and tt == 0 else
                        [bank_fill[bank]])
                    bank_free[bank] = dsg
                    wslot_free[sl] = dmm
                    setrd(wbuf, dmm)
            return dsg

        def attn_tail(chunk, dzlast):
            wslot2 = ATT["proj_w"]
            dwb = None
            for half in range(2):
                dxts = [None, None]
                dadds = [None, None]
                for pc in range(8):
                    sl = pc % 2
                    dwd = dma("gpsimd", f"gwl{sl}",
                              wbuf[:, sl * 4096:(sl + 1) * 4096]
                              .rearrange("p (n f) -> p n f", n=16),
                              wpd[:, pc * 256:(pc + 1) * 256]
                              .rearrange("(n p) f -> p n f", p=128),
                              waits=[wslot2[sl], rd(wbuf)])
                    for tt2 in range(2):
                        tt = half * 2 + tt2
                        t = chunk * 4 + tt
                        if pc == 0:
                            dxts[tt2] = load_xtile(t)
                        bank = (pc * 2 + tt2) % 3
                        dmm = None
                        for hh in range(16):
                            dmm = comp("tensor", lambda E, hh=hh, tt=tt,
                                       bank=bank, sl=sl: E.matmul(
                                pb[bank][:, 0:256],
                                gch[:, tt * 2048 + hh * 128:
                                    tt * 2048 + (hh + 1) * 128],
                                wbuf[:, sl * 4096 + hh * 256:
                                     sl * 4096 + (hh + 1) * 256],
                                start=(hh == 0), stop=(hh == 15)),
                                waits=[dwd, bank_free[bank], dzlast]
                                if hh == 0 else [])
                        bank_fill[bank] = dmm
                        setrd(gch, dmm)
                        setrd(wbuf, dmm)
                        slot = dxts[tt2][1]
                        dadds[tt2] = comp("vector",
                                          lambda E, bank=bank, pc=pc, slot=slot:
                                          E.scalar_tensor_tensor(
                            xt[slot][:, pc * 256:(pc + 1) * 256],
                            pb[bank][:, 0:256], 1.0,
                            xt[slot][:, pc * 256:(pc + 1) * 256],
                            ALU.mult, ALU.add),
                            waits=[bank_fill[bank], dxts[tt2][0], dadds[tt2]])
                        bank_free[bank] = dadds[tt2]
                        wslot2[sl] = dmm
                for tt2 in range(2):
                    tt = half * 2 + tt2
                    t = chunk * 4 + tt
                    slot = dxts[tt2][1]
                    dwb = dma("scalar", "xw2", x_dram[t * 128:(t + 1) * 128, :],
                              xt[slot][:], waits=[dadds[tt2], dwb])
                    xt_free[slot] = dwb
            return

        if DO_FFN1:
            ffn_phase(x_dram, "xw1")
            phase_entry[0] = (SM["xw1"], sch.n(SM["xw1"]))
        if DO_ATTN:
            attn_phase()
            phase_entry[0] = (SM["xw2"], sch.n(SM["xw2"]))
        if DO_FFN2:
            ffn_phase(out_c, "xw3")
        else:
            dma("gpsimd", "xw3", out_c, x_dram,
                waits=[phase_entry[0]])
        if DBG:
            dma("sync", "cst", dbg_st[:, 0:96], st[:, 0:96],
                waits=[(vS, sch.n(vS)), (sS, sch.n(sS)), (tS, sch.n(tS))])
            dma("sync", "cst", dbg_row[0:1, 128:135], srow[0:1, 128:135])
            dma("sync", "cst", dbg_bc[:, 0:6], bc[:, 0:6])
            dma("sync", "cst", dbg_afm, fm[:, 0:16384])
            dma("gpsimd", "gdbg", dbg_h, h_dram[0:128, :])
        final_waits = [(s, sch.n(s)) for s in SM.values()]
        final_waits += [(vS, sch.n(vS)), (sS, sch.n(sS)), (tS, sch.n(tS)),
                        (gS, sch.n(gS)), (ccS, sch.n(ccS))]
        sch.add("sync", lambda E: E.nop(), waits=final_waits)

        @block.sync
        def _(E):
            sch.replay("sync", E)

        @block.gpsimd
        def _(E):
            sch.replay("gpsimd", E)

        @block.vector
        def _(E):
            sch.replay("vector", E)

        @block.scalar
        def _(E):
            sch.replay("scalar", E)

        @block.tensor
        def _(E):
            sch.replay("tensor", E)

    return nc


def kernel(L0_attn_x, L0_attn_w_qkv, L0_attn_w_g, L0_attn_w_proj,
           L0_attn_gn_weight, L0_attn_gn_bias, L0_ffn_w1, L0_ffn_w2, L0_ffn_w3):
    x = np.asarray(L0_attn_x, dtype=np.float32)
    shared = _shared_inputs(np.asarray(L0_attn_w_qkv, np.float32),
                            np.asarray(L0_attn_w_g, np.float32),
                            np.asarray(L0_attn_w_proj, np.float32),
                            np.asarray(L0_attn_gn_weight, np.float32),
                            np.asarray(L0_attn_gn_bias, np.float32))
    in_maps = [
        _host_inputs(core, x,
                     np.asarray(L0_ffn_w1, np.float32),
                     np.asarray(L0_ffn_w2, np.float32),
                     np.asarray(L0_ffn_w3, np.float32), shared)
        for core in range(NCORES)
    ]
    nc = build_nc()
    res = run_bass_kernel_spmd(nc, in_maps, core_ids=list(range(NCORES)))
    out = np.zeros((B, T, C), np.float32)
    for core in range(NCORES):
        b = core // 4
        cA, cB = _chunks_of(core)
        oc = np.asarray(res.results[core]["out_c"])
        out[b, cA * 512:(cA + 1) * 512] = oc[0:512]
        out[b, cB * 512:(cB + 1) * 512] = oc[512:1024]
    return out


# version 10 (final)
# speedup vs baseline: 1.0421x; 1.0129x over previous
import math
import numpy as np
import ml_dtypes
import concourse.bass as bass
import concourse.mybir as mybir
from concourse.bass_utils import run_bass_kernel_spmd
from contextlib import ExitStack

F32 = mybir.dt.float32
BF16 = mybir.dt.bfloat16
ALU = mybir.AluOpType
AF = mybir.ActivationFunctionType
AX = mybir.AxisListType.X

B, T, C, H, D, DFF = 2, 4096, 2048, 16, 128, 8192
NCORES = 8
NT = 8
TOK = NT * 128
MAGIC = float(np.float32(1.5 * 2 ** 23))
SCALE_BASE = 512.0
NELEM_W = float(DFF * C)

DO_FFN1 = True
DBG = False
DO_ATTN = True
DO_FFN2 = True

PASS_LIST = [
    (0, ("diag",)), (0, ("slot", 0, 0)), (0, ("slot", 1, 0)), (0, ("slot", 2, 0)),
    (1, ("diag",)), (1, ("slot", 0, 0)), (1, ("slot", 1, 0)), (1, ("slot", 2, 0)),
    (1, ("slot", 3, 0)), (1, ("slot", 3, 1)), (1, ("slot", 2, 1)), (1, ("slot", 1, 1)),
]
NPASS = len(PASS_LIST)


def _gammas():
    return 1.0 - 2.0 ** (-5.0 - np.arange(H, dtype=np.float64))


def _chunks_of(core):
    r = core % 4
    return r, 7 - r


def _xpos_tables(positions):
    dim_scale = (np.arange(0, C, 2, dtype=np.float64) + 0.4 * C) / (1.4 * C)
    pos = positions.astype(np.float64)
    scale = dim_scale[None, :] ** (pos[:, None] / SCALE_BASE)
    inv_freq = 1.0 / 10000.0 ** (np.arange(0, C, 2, dtype=np.float64) / C)
    sinu = pos[:, None] * inv_freq[None, :]
    sin = np.repeat(np.sin(sinu) * scale, 2, axis=-1)
    cos = np.repeat(np.cos(sinu) * scale, 2, axis=-1)
    bs = sin.copy()
    bs[:, 0::2] = -sin[:, 0::2]
    return cos, bs


def _sctab_for_core(core):
    cA, cB = _chunks_of(core)
    gam = _gammas()
    tab = np.zeros((H, NPASS, 4), dtype=np.float64)
    for p, (qc, src) in enumerate(PASS_LIST):
        c_own = cA if qc == 0 else cB
        tc0 = c_own * 512
        if src[0] == "diag":
            s_chunk, valid = c_own, True
        else:
            slot, sub = src[1], src[2]
            s_chunk = slot if sub == 0 else 7 - slot
            valid = s_chunk < c_own
        if not valid:
            continue
        for q in range(4):
            s0 = s_chunk * 512 + q * 128
            for h in range(H):
                tab[h, p, q] = gam[h] ** (tc0 - s0)
    with np.errstate(under="ignore"):
        flat = tab.reshape(1, -1).astype(np.float32)
    return np.ascontiguousarray(np.broadcast_to(flat, (128, flat.shape[1])))


def _shared_inputs(w_qkv, w_g, w_proj, gnw, gnb):
    gam = _gammas()
    p_ = np.arange(128)
    qdc = np.zeros((128, H * 4), np.float64)
    for h in range(H):
        for j in range(4):
            qdc[:, h * 4 + j] = gam[h] ** (128 * j + p_) / math.sqrt(D)
    kdc = np.zeros((128, H), np.float64)
    for h in range(H):
        kdc[:, h] = gam[h] ** (-p_.astype(np.float64))
    tri = np.zeros((128, 4 * 512), np.float32)
    f_ = np.arange(512)
    for q in range(4):
        tri[:, q * 512:(q + 1) * 512] = (
            f_[None, :] >= (p_[:, None] + 128 * q)).astype(np.float32)
    return dict(
        wqkv=np.ascontiguousarray(w_qkv, np.float32),
        wg=np.ascontiguousarray(w_g, np.float32),
        wproj=np.ascontiguousarray(w_proj, np.float32),
        gnw_r=np.ascontiguousarray(gnw.reshape(H, 128).T, np.float32),
        gnb_r=np.ascontiguousarray(gnb.reshape(H, 128).T, np.float32),
        qdec=qdc.astype(np.float32),
        kdec=kdc.astype(np.float32),
        tri=tri.astype(ml_dtypes.bfloat16),
        ident=np.eye(128, dtype=ml_dtypes.bfloat16),
        identf=np.eye(128, dtype=np.float32),
        ones_r=np.ones((1, 128), np.float32),
    )


def _host_inputs(core, x, w1, w2, w3, shared):
    b = core // 4
    cA, cB = _chunks_of(core)
    xs = np.concatenate(
        [x[b, cA * 512:(cA + 1) * 512], x[b, cB * 512:(cB + 1) * 512]], axis=0)
    pos = np.concatenate(
        [np.arange(cA * 512, (cA + 1) * 512), np.arange(cB * 512, (cB + 1) * 512)])
    cos, bs = _xpos_tables(pos)
    d = dict(
        x_in=np.ascontiguousarray(xs, dtype=np.float32),
        w1s=np.ascontiguousarray(w1.T[core * 256:(core + 1) * 256]),
        w2s=np.ascontiguousarray(w2.T[core * 256:(core + 1) * 256]),
        w3s=np.ascontiguousarray(w3.T[core * 1024:(core + 1) * 1024]),
        cosb=cos.astype(np.float32),
        sinb=bs.astype(np.float32),
        sctab=_sctab_for_core(core),
    )
    d.update(shared)
    return d


class Sched:
    def __init__(self):
        self.steps = []
        self.cnt = {}

    def n(self, sem):
        return self.cnt.get(id(sem), 0)

    def add(self, eng, fn, waits=(), inc=None, incv=1):
        if inc is not None:
            self.cnt[id(inc)] = self.n(inc) + incv
        seen = {}
        for w in waits:
            if w is None:
                continue
            s, v = w
            if id(s) in seen:
                seen[id(s)] = (s, max(seen[id(s)][1], v))
            else:
                seen[id(s)] = (s, v)
        self.steps.append((eng, fn, list(seen.values()),
                           [(inc, incv)] if inc is not None else []))

    def replay(self, name, eng):
        for en, fn, waits, incs in self.steps:
            if en != name:
                continue
            for sem, v in waits:
                eng.wait_ge(sem, v)
            instr = fn(eng)
            for sem, v in incs:
                instr.then_inc(sem, v)


def build_nc():
    nc = bass.Bass("TRN2", target_bir_lowering=False)
    ap = lambda t: t.ap()

    x_in = ap(nc.dram_tensor("x_in", [TOK, C], F32, kind="ExternalInput"))
    wqkv = ap(nc.dram_tensor("wqkv", [C, 3 * C], F32, kind="ExternalInput"))
    wgd = ap(nc.dram_tensor("wg", [C, C], F32, kind="ExternalInput"))
    wpd = ap(nc.dram_tensor("wproj", [C, C], F32, kind="ExternalInput"))
    gnw_in = ap(nc.dram_tensor("gnw_r", [128, H], F32, kind="ExternalInput"))
    gnb_in = ap(nc.dram_tensor("gnb_r", [128, H], F32, kind="ExternalInput"))
    w1s = ap(nc.dram_tensor("w1s", [256, DFF], F32, kind="ExternalInput"))
    w2s = ap(nc.dram_tensor("w2s", [256, DFF], F32, kind="ExternalInput"))
    w3s = ap(nc.dram_tensor("w3s", [1024, C], F32, kind="ExternalInput"))
    cosb = ap(nc.dram_tensor("cosb", [TOK, C], F32, kind="ExternalInput"))
    sinb = ap(nc.dram_tensor("sinb", [TOK, C], F32, kind="ExternalInput"))
    qdec = ap(nc.dram_tensor("qdec", [128, H * 4], F32, kind="ExternalInput"))
    kdec = ap(nc.dram_tensor("kdec", [128, H], F32, kind="ExternalInput"))
    tri_in = ap(nc.dram_tensor("tri", [128, 4 * 512], BF16, kind="ExternalInput"))
    sctab_in = ap(nc.dram_tensor("sctab", [128, H * NPASS * 4], F32,
                                 kind="ExternalInput"))
    ident_in = ap(nc.dram_tensor("ident", [128, 128], BF16, kind="ExternalInput"))
    identf_in = ap(nc.dram_tensor("identf", [128, 128], F32, kind="ExternalInput"))
    ones_in = ap(nc.dram_tensor("ones_r", [1, 128], F32, kind="ExternalInput"))

    out_c = ap(nc.dram_tensor("out_c", [TOK, C], F32, kind="ExternalOutput"))
    if DBG:
        dbg_st = ap(nc.dram_tensor("dbg_st", [128, 384], F32, kind="ExternalOutput"))
        dbg_row = ap(nc.dram_tensor("dbg_row", [1, 144], F32, kind="ExternalOutput"))
        dbg_bc = ap(nc.dram_tensor("dbg_bc", [128, 8], F32, kind="ExternalOutput"))
        dbg_afm = ap(nc.dram_tensor("dbg_afm", [128, 16384], BF16, kind="ExternalOutput"))
        dbg_h = ap(nc.dram_tensor("dbg_h", [128, DFF], F32, kind="ExternalOutput"))

    x_dram = ap(nc.dram_tensor("x_dram", [TOK, C], F32, kind="Internal"))
    h_dram = ap(nc.dram_tensor("h_dram", [TOK, DFF], F32, kind="Internal"))
    ag1_in_j = [ap(nc.dram_tensor(f"ag1i{j}", [512, 2048], BF16,
                                  kind="Internal")) for j in range(4)]
    ag1_out_j = [ap(nc.dram_tensor(f"ag1o{j}", [NCORES * 512, 2048], BF16,
                                   kind="Internal")) for j in range(4)]
    ag2_in = ap(nc.dram_tensor("ag2_in", [1024, C], BF16, kind="Internal"))
    ag2_out = ap(nc.dram_tensor("ag2_out", [NCORES * 1024, C], BF16, kind="Internal"))
    ar_in = ap(nc.dram_tensor("ar_in", [1, 4], F32, kind="Internal"))
    ar_out = ap(nc.dram_tensor("ar_out", [1, 4], F32, kind="Internal"))
    pflip = ap(nc.dram_tensor("pflip", [128, 1], F32, kind="Internal"))
    k_in = ap(nc.dram_tensor("k_in", [H * 128, TOK], BF16, kind="Internal"))
    k_out = ap(nc.dram_tensor("k_out", [4 * H * 128, TOK], BF16, kind="Internal"))
    v_in = ap(nc.dram_tensor("v_in", [H * TOK, D], BF16, kind="Internal"))
    v_out = ap(nc.dram_tensor("v_out", [4 * H * TOK, D], BF16, kind="Internal"))

    def w1t_rows(k):
        return (k // 2) * 512 + (k % 2) * 128

    def w2t_rows(k):
        return (k // 2) * 512 + (k % 2) * 128 + 256

    def w3t_rows(k):
        return (k // 8) * 1024 + (k % 8) * 128

    sch = Sched()

    with ExitStack() as ctx:
        e = ctx.enter_context
        xt = [e(nc.sbuf_tensor(f"xt{i}", [128, C], F32)) for i in range(2)]
        fm = e(nc.sbuf_tensor("fm", [128, 32768], BF16))
        wbuf = e(nc.sbuf_tensor("wbuf", [128, 8192], BF16))
        big3 = e(nc.sbuf_tensor("big3", [128, 4096], F32))
        aux = e(nc.sbuf_tensor("aux", [128, 2048], BF16))
        kst = e(nc.sbuf_tensor("kst", [128, 3 * 512], BF16))
        vst = e(nc.sbuf_tensor("vst", [128, 3 * 512], BF16))
        csb = e(nc.sbuf_tensor("csb", [128, 1536], F32))
        w3b = e(nc.sbuf_tensor("w3b", [128, 6144], BF16))
        hstg = e(nc.sbuf_tensor("hstg", [128, 2048], F32))
        smb = e(nc.sbuf_tensor("smb", [128, 4 * 512], BF16))
        qst = e(nc.sbuf_tensor("qst", [128, 2048], F32))
        gch = e(nc.sbuf_tensor("gch", [128, DFF], BF16))
        sqs = e(nc.sbuf_tensor("sqs", [128, C], F32))
        qraw = e(nc.sbuf_tensor("qraw", [128, 2048], F32))
        tri = e(nc.sbuf_tensor("tri_s", [128, 4 * 512], BF16))
        gnw = e(nc.sbuf_tensor("gnw_s", [128, H], F32))
        gnb = e(nc.sbuf_tensor("gnb_s", [128, H], F32))
        sct = e(nc.sbuf_tensor("sct_s", [128, H * NPASS * 4], F32))
        qde = e(nc.sbuf_tensor("qde_s", [128, H * 4], F32))
        kde = e(nc.sbuf_tensor("kde_s", [128, H], F32))
        idn = e(nc.sbuf_tensor("idn_s", [128, 128], BF16))
        idnf = e(nc.sbuf_tensor("idnf_s", [128, 128], F32))
        ones1 = e(nc.sbuf_tensor("ones1", [1, 128], F32))
        st = e(nc.sbuf_tensor("stats", [128, 384], F32))
        sv = e(nc.sbuf_tensor("sv", [128, 16], F32))
        srow = e(nc.sbuf_tensor("srow", [1, 144], F32))
        bc = e(nc.sbuf_tensor("bc", [128, 8], F32))

        pb = [e(nc.psum_tensor(f"pb{i}", [128, 512], F32)) for i in range(6)]
        ptr2 = [e(nc.psum_tensor(f"ptr{i}", [128, 1024], BF16)) for i in range(2)]

        sem_names = ["cst", "xcp", "bl", "p0f", "agw", "wl0", "wl1",
                     "xl0", "xl1", "hs0", "hs1", "hrl", "w3m0", "w3m1",
                     "w3t0", "w3t1", "csm0", "csm1", "csm2", "csm3", "hs2", "hs3",
                     "hs4", "hs5", "hs6", "hs7", "xl2", "xl3", "bl2", "agw2",
                     "w3a0", "w3a1", "w3a2", "w3a3", "w3a4", "w3a5",
                     "qw0", "qw1",
                     "klm0", "klm1", "klm2", "vlm0", "vlm1", "vlm2",
                     "kvw", "xw1", "xw2", "xw3", "gwl0", "gwl1", "gdbg"]
        SM = {nm: e(nc.semaphore(nm)) for nm in sem_names}
        vS = e(nc.semaphore("vS"))
        sS = e(nc.semaphore("sS"))
        tS = e(nc.semaphore("tS"))
        gS = e(nc.semaphore("gS"))
        ccS = e(nc.semaphore("ccS"))

        block = e(nc.Block())

        ESEM = {"vector": vS, "scalar": sS, "gpsimd": gS, "tensor": tS}

        def dma(queue, sem, out, in_, waits=()):
            s = SM[sem]
            sch.add(queue, lambda E, o=out, i=in_: E.dma_start(out=o, in_=i),
                    waits=waits, inc=s, incv=16)
            return (s, sch.n(s))

        def comp(eng, fn, waits=()):
            s = ESEM[eng]
            sch.add(eng, fn, waits=waits, inc=s)
            return (s, sch.n(s))

        def coll(fn, waits=()):
            sch.add("gpsimd", fn, waits=waits, inc=ccS, incv=1)
            return (ccS, sch.n(ccS))

        bank_fill = [(tS, 0)] * 6
        bank_free = [None] * 6
        ptr_fill = [(tS, 0)] * 2
        ptr_free = [None, None]
        buf_rd = {}

        def rd(buf):
            return buf_rd.get(id(buf))

        def setrd(buf, dep):
            old = buf_rd.get(id(buf))
            if old is not None and old[0] is dep[0]:
                dep = (dep[0], max(dep[1], old[1]))
            buf_rd[id(buf)] = dep

        for dst, src in ((tri, tri_in), (gnw, gnw_in), (gnb, gnb_in),
                         (sct, sctab_in), (qde, qdec), (kde, kdec),
                         (idn, ident_in), (idnf, identf_in),
                         (ones1, ones_in)):
            dma("sync", "cst", dst[:], src)
        dep_consts = (SM["cst"], sch.n(SM["cst"]))
        dma("gpsimd", "xcp", x_dram, x_in)
        dep_xcopy = (SM["xcp"], sch.n(SM["xcp"]))

        vhalf = [None, None]
        p0pi = [0]

        def p0_sum(slc, rows, cols, col):
            dvl = None
            pieces = [(r, c0) for r in range(rows // 128)
                      for c0 in range(0, cols, 2048)]
            for pi, (r, c0) in enumerate(pieces):
                half = p0pi[0] % 2
                p0pi[0] += 1
                dd = dma("sync" if half == 0 else "scalar",
                         "bl" if half == 0 else "bl2",
                         big3[:, half * 2048:(half + 1) * 2048],
                         slc[r * 128:(r + 1) * 128, c0:c0 + 2048],
                         waits=[vhalf[half]])
                sc = 4 + col + half * 3
                tgt = st[:, col:col + 1] if pi == 0 else st[:, sc:sc + 1]
                dvl = comp("vector",
                           lambda E, tg=tgt, half=half: E.reduce_sum(
                               tg, big3[:, half * 2048:(half + 1) * 2048],
                               AX, apply_absolute_value=True),
                           waits=[dd])
                if pi > 0:
                    dvl = comp("vector", lambda E, c=col, sc=sc: E.tensor_add(
                        st[:, c:c + 1], st[:, c:c + 1], st[:, sc:sc + 1]),
                        waits=[dvl])
                vhalf[half] = dvl
            return dvl

        d1 = p0_sum(w1s, 256, DFF, 0)
        d2 = p0_sum(w2s, 256, DFF, 1)
        d3 = p0_sum(w3s, 1024, C, 2)
        rdep = None
        for j, dep in enumerate((d1, d2, d3)):
            f1 = dma("scalar", "p0f", pflip, st[:, j:j + 1], waits=[dep, rdep])
            f2 = dma("scalar", "p0f", srow[0:1, 0:128],
                     pflip.rearrange("p one -> one p"), waits=[f1])
            rdep = comp("vector", lambda E, j=j: E.reduce_sum(
                srow[0:1, 128 + j:129 + j], srow[0:1, 0:128], AX), waits=[f2])
        dz = comp("vector", lambda E: E.memset(srow[0:1, 131:132], 0.0))
        da = dma("scalar", "p0f", ar_in, srow[0:1, 128:132], waits=[rdep, dz])
        dar = coll(lambda E: E.collective_compute(
            "AllReduce", ALU.add, replica_groups=[list(range(NCORES))],
            ins=[ar_in.opt()], outs=[ar_out.opt()]), waits=[da])
        dld = dma("scalar", "p0f", srow[0:1, 136:140], ar_out, waits=[dar])
        dmw = comp("vector", lambda E: E.tensor_scalar(
            srow[0:1, 132:135], srow[0:1, 136:139], 1.0 / NELEM_W / 127.0, None,
            ALU.mult), waits=[dld])
        dmm_ = comp("vector", lambda E: E.tensor_scalar(
            srow[0:1, 136:139], srow[0:1, 136:139], 1.0 / NELEM_W, None,
            ALU.mult), waits=[dmw])
        div = comp("vector", lambda E: E.reciprocal(
            srow[0:1, 140:143], srow[0:1, 136:139]), waits=[dmm_])
        dg1 = comp("vector", lambda E: E.tensor_copy(
            srow[0:1, 0:3], srow[0:1, 132:135]), waits=[div])
        dg2 = comp("vector", lambda E: E.tensor_copy(
            srow[0:1, 3:6], srow[0:1, 140:143]), waits=[dg1])
        dbc = comp("tensor", lambda E: E.matmul(
            pb[5][:, 0:6], ones1[0:1, :], srow[0:1, 0:6], start=True, stop=True),
            waits=[dg2, dep_consts, bank_free[5]])
        bank_fill[5] = dbc
        dbc2 = comp("vector", lambda E: E.tensor_copy(bc[:, 0:6], pb[5][:, 0:6]),
                    waits=[dbc])
        bank_free[5] = dbc2

        wprev = [None, None]

        def tern_piece(slc, r, c0, ivcol, dst):
            half = p0pi[0] % 2
            p0pi[0] += 1
            dd = dma("sync" if half == 0 else "scalar",
                     "bl" if half == 0 else "bl2",
                     big3[:, half * 2048:(half + 1) * 2048],
                     slc[r * 128:(r + 1) * 128, c0:c0 + 2048],
                     waits=[dbc2, vhalf[half]])
            dv = comp("vector", lambda E, half=half, c=ivcol: E.tensor_scalar(
                big3[:, half * 2048:(half + 1) * 2048],
                big3[:, half * 2048:(half + 1) * 2048], bc[:, c:c + 1], MAGIC,
                ALU.mult, ALU.add), waits=[dd])
            dv = comp("vector", lambda E, half=half: E.tensor_scalar(
                big3[:, half * 2048:(half + 1) * 2048],
                big3[:, half * 2048:(half + 1) * 2048], MAGIC, 1.0,
                ALU.subtract, ALU.min), waits=[dv])
            dv = comp("vector", lambda E, half=half: E.tensor_scalar(
                wbuf[:, 4096 + half * 2048:4096 + (half + 1) * 2048],
                big3[:, half * 2048:(half + 1) * 2048], -1.0, None, ALU.max),
                waits=[dv, wprev[half]])
            vhalf[half] = dv
            dw = dma("scalar", "agw" if half == 0 else "agw2", dst,
                     wbuf[:, 4096 + half * 2048:4096 + (half + 1) * 2048],
                     waits=[dv, wprev[half]])
            wprev[half] = dw
            setrd(wbuf, dw)
            setrd(big3, dv)
            return dw

        dag1s = []
        for j in range(4):
            jdeps = []
            for slc, ivcol, base in ((w1s, 3, 0), (w2s, 4, 256)):
                for r in range(2):
                    jdeps.append(tern_piece(
                        slc, r, j * 2048, ivcol,
                        ag1_in_j[j][base + r * 128:base + (r + 1) * 128, :]))
            dag1s.append(coll(lambda E, j=j: E.collective_compute(
                "AllGather", ALU.bypass, replica_groups=[list(range(NCORES))],
                ins=[ag1_in_j[j].opt()], outs=[ag1_out_j[j].opt()]),
                waits=jdeps))
        dt3 = None
        for r in range(8):
            dt3 = tern_piece(w3s, r, 0, 5, ag2_in[r * 128:(r + 1) * 128, :])
        dag2 = coll(lambda E: E.collective_compute(
            "AllGather", ALU.bypass, replica_groups=[list(range(NCORES))],
            ins=[ag2_in.opt()], outs=[ag2_out.opt()]),
            waits=[wprev[0], wprev[1]])
        tern_wdep = [wprev[0], wprev[1]]

        xt_free = [None, None]
        phase_entry = [dep_xcopy]

        def load_xtile(t, src=x_dram):
            slot = t % 2
            d = dma("sync", f"xl{slot}", xt[slot][:], src[t * 128:(t + 1) * 128, :],
                    waits=[xt_free[slot], phase_entry[0]])
            return d, slot

        def ln_stats(t, slot, dxt, want_minmax):
            b = t * 12
            dsum = comp("vector", lambda E, b=b, slot=slot: E.reduce_sum(
                st[:, b:b + 1], xt[slot][:], AX), waits=[dxt])
            dsq = comp("scalar", lambda E, b=b, slot=slot: E.activation(
                sqs[:], xt[slot][:], AF.Square, accum_out=st[:, b + 1:b + 2]),
                waits=[dxt, rd(sqs)])
            setrd(sqs, dsq)
            deps = [dsum, dsq]
            if want_minmax:
                deps.append(comp("vector", lambda E, b=b, slot=slot:
                                 E.tensor_reduce(
                    st[:, b + 2:b + 3], xt[slot][:], AX, ALU.max), waits=[dxt]))
                deps.append(comp("vector", lambda E, b=b, slot=slot:
                                 E.tensor_reduce(
                    st[:, b + 3:b + 4], xt[slot][:], AX, ALU.min), waits=[dxt]))
            return deps

        def transpose16(src_bf, dsrc, dst_of, nt_=16, extra_waits=()):
            dv = None
            for h0 in range(0, nt_, 8):
                n = min(8, nt_ - h0)
                half = (h0 // 8) % 2
                pt = ptr2[half]
                dt = None
                for j in range(n):
                    dt = comp("tensor", lambda E, j=j, jj=h0 + j, pt=pt:
                              E.transpose(
                        pt[:, j * 128:(j + 1) * 128],
                        src_bf[:, jj * 128:(jj + 1) * 128], idn[:]),
                        waits=[dsrc, ptr_free[half], dep_consts] if j == 0
                        else [dsrc])
                    ptr_fill[half] = dt
                dv = comp("vector", lambda E, h0=h0, n=n, pt=pt:
                          E.tensor_copy(
                    dst_of(h0, n),
                    pt[:, 0:n * 128]
                    .rearrange("p (nn z) -> p nn z", z=128)),
                    waits=[dt] + list(extra_waits))
                ptr_free[half] = dv
            return dv

        def fm_dst(buf, stride, base, toff):
            def f(j0, n, buf=buf, stride=stride, base=base, toff=toff):
                v = buf[:].rearrange("p (ct z) -> p ct z", z=stride)
                return v[:, base + j0: base + j0 + n, toff:toff + 128]
            return f

        def ffn_phase(dst, wsem):
            dcoefs = [None] * NT
            for t in range(NT):
                dxt, slot = load_xtile(t)
                deps = ln_stats(t, slot, dxt, True)
                b = t * 12
                cS = lambda i, b=b: st[:, b + i:b + i + 1]
                d = comp("vector", lambda E, cS=cS: E.tensor_scalar(
                    cS(9), cS(0), 1.0 / C, None, ALU.mult), waits=deps)
                d = comp("vector", lambda E, cS=cS: E.tensor_scalar(
                    cS(10), cS(1), 1.0 / C, None, ALU.mult), waits=[d])
                d = comp("vector", lambda E, cS=cS: E.scalar_tensor_tensor(
                    cS(11), cS(9), 1.0, cS(9), ALU.mult, ALU.mult), waits=[d])
                d = comp("vector", lambda E, cS=cS: E.tensor_sub(
                    cS(10), cS(10), cS(11)), waits=[d])
                d = comp("vector", lambda E, cS=cS: E.tensor_scalar(
                    cS(11), cS(10), 1.000001, 1e-11, ALU.mult, ALU.add), waits=[d])
                d = comp("scalar", lambda E, cS=cS: E.activation(
                    cS(11), cS(11), AF.Sqrt), waits=[d])
                d = comp("vector", lambda E, cS=cS: E.reciprocal(cS(11), cS(11)),
                         waits=[d])
                d2 = comp("vector", lambda E, cS=cS: E.tensor_sub(
                    cS(2), cS(2), cS(9)), waits=[d])
                d2 = comp("vector", lambda E, cS=cS: E.tensor_sub(
                    cS(3), cS(9), cS(3)), waits=[d2])
                d2 = comp("vector", lambda E, cS=cS: E.tensor_tensor(
                    cS(10), cS(2), cS(3), ALU.max), waits=[d2])
                d2 = comp("vector", lambda E, cS=cS: E.tensor_mul(
                    cS(8), cS(10), cS(11)), waits=[d2])
                d2 = comp("vector", lambda E, cS=cS: E.tensor_scalar(
                    cS(8), cS(8), 1e-5, None, ALU.max), waits=[d2])
                d2 = comp("vector", lambda E, cS=cS: E.reciprocal(
                    cS(2), cS(8)), waits=[d2])
                d2 = comp("vector", lambda E, cS=cS: E.tensor_mul(
                    cS(4), cS(11), cS(2)), waits=[d2])
                d2 = comp("vector", lambda E, cS=cS: E.tensor_scalar(
                    cS(4), cS(4), 127.0, None, ALU.mult), waits=[d2])
                d2 = comp("vector", lambda E, cS=cS: E.scalar_tensor_tensor(
                    cS(5), cS(9), -1.0, cS(4), ALU.mult, ALU.mult), waits=[d2])
                d2 = comp("vector", lambda E, cS=cS: E.tensor_mul(
                    cS(6), cS(8), bc[:, 0:1]), waits=[d2, dbc2])
                d2 = comp("vector", lambda E, cS=cS: E.tensor_mul(
                    cS(7), cS(8), bc[:, 1:2]), waits=[d2])
                du = comp("scalar", lambda E, slot=slot, cS=cS: E.activation(
                    sqs[:], xt[slot][:], AF.Identity, bias=cS(5), scale=cS(4)),
                    waits=[d2, rd(sqs)])
                du = comp("vector", lambda E: E.tensor_scalar(
                    sqs[:], sqs[:], MAGIC, MAGIC, ALU.add, ALU.subtract),
                    waits=[du])
                du = comp("vector", lambda E: E.tensor_scalar(
                    aux[:], sqs[:], 127.0, -128.0, ALU.min, ALU.max),
                    waits=[du, rd(aux)])
                setrd(sqs, du)
                dq = transpose16(aux, du, fm_dst(fm, 1024, 0, t * 128),
                                 extra_waits=[rd(fm)])
                setrd(aux, dq)
                xt_free[slot] = du
                dm0 = comp("vector", lambda E, cS=cS: E.memset(cS(2), 0.0),
                           waits=[d2])
                dm1 = comp("vector", lambda E, cS=cS: E.memset(cS(3), 0.0),
                           waits=[d2])
                dcoefs[t] = (d2, dm0, dm1)
                afm_all = dq

            wslot_free = [None, None]
            hsa_free = [None] * NT
            hslot_free = hslot_free_attn
            last_mm12 = None
            for o in range(DFF // 256):
                for w12 in range(2):
                    ci = 2 * o + w12
                    sl = ci % 2
                    rows = w1t_rows if w12 == 0 else w2t_rows
                    dwl = None
                    for kk in range(0, 16, 2):
                        dwl = dma(
                            "sync", f"wl{sl}",
                            wbuf[:, sl * 4096 + kk * 256:sl * 4096 + (kk + 2) * 256]
                            .rearrange("p (n z) -> p n z", n=2),
                            ag1_out_j[o // 8][rows(kk):rows(kk) + 256,
                                              (o % 8) * 256:(o % 8 + 1) * 256]
                            .rearrange("(n p) z -> p n z", p=128),
                            waits=[wslot_free[sl], dag1s[o // 8], rd(wbuf)]
                            + tern_wdep if kk == 0 else [])
                    dmm = None
                    for t in range(NT):
                        bank = t % 4
                        b = t * 12
                        for k in range(16):
                            dmm = comp("tensor", lambda E, k=k, t=t, bank=bank,
                                       sl=sl: E.matmul(
                                pb[bank][:, 0:256],
                                fm[:, k * 1024 + t * 128:k * 1024 + (t + 1) * 128],
                                wbuf[:, sl * 4096 + k * 256:sl * 4096 + (k + 1) * 256],
                                start=(k == 0), stop=(k == 15)),
                                waits=[dwl, bank_free[bank], afm_all,
                                       dcoefs[t][0]] if k == 0 else [])
                        bank_fill[bank] = dmm
                        last_mm12 = dmm
                        if w12 == 0:
                            dsil = comp("scalar", lambda E, bank=bank, b=b, t=t:
                                        E.activation(
                                qst[:, t * 256:(t + 1) * 256],
                                pb[bank][:, 0:256], AF.Silu,
                                scale=st[:, b + 6:b + 7]),
                                waits=[bank_fill[bank], hsa_free[t]])
                            bank_free[bank] = dsil
                            hsa_free[t] = dsil
                        else:
                            hslot = (o * 8 + t) % 8
                            dh = comp("vector", lambda E, bank=bank, b=b, t=t,
                                      hslot=hslot:
                                      E.scalar_tensor_tensor(
                                hstg[:, hslot * 256:hslot * 256 + 256],
                                pb[bank][:, 0:256], st[:, b + 7:b + 8],
                                qst[:, t * 256:(t + 1) * 256],
                                ALU.mult, ALU.mult),
                                waits=[bank_fill[bank], hsa_free[t],
                                       hslot_free[hslot]])
                            bank_free[bank] = dh
                            hsa_free[t] = dh
                            dv = comp("vector", lambda E, hslot=hslot:
                                      E.reduce_max(
                                sv[:, 0:1],
                                hstg[:, hslot * 256:hslot * 256 + 256],
                                AX, apply_absolute_value=True), waits=[dh])
                            dv = comp("vector", lambda E, b=b: E.tensor_tensor(
                                st[:, b + 2:b + 3], st[:, b + 2:b + 3],
                                sv[:, 0:1], ALU.max), waits=[dv, dcoefs[t][1]])
                            dg = comp("scalar", lambda E, hslot=hslot,
                                      t=t: E.activation(
                                aux[:, 0:256],
                                hstg[:, hslot * 256:hslot * 256 + 256],
                                AF.Square,
                                accum_out=sv[:, 4 + (t % 4):5 + (t % 4)]),
                                waits=[dh, rd(aux)])
                            setrd(aux, dg)
                            dg = comp("vector", lambda E, b=b, t=t: E.tensor_add(
                                st[:, b + 3:b + 4], st[:, b + 3:b + 4],
                                sv[:, 4 + (t % 4):5 + (t % 4)]),
                                waits=[dg, dcoefs[t][2]])
                            dsp = dma("scalar", f"hs{hslot}",
                                      h_dram[t * 128:(t + 1) * 128,
                                             o * 256:(o + 1) * 256],
                                      hstg[:, hslot * 256:hslot * 256 + 256],
                                      waits=[dh, dv, dg])
                            hslot_free[hslot] = dsp
                    wslot_free[sl] = dmm
            setrd(fm, last_mm12)
            setrd(wbuf, last_mm12)
            hdeps = [None] * NT
            for t in range(NT):
                b = t * 12
                cS = lambda i, b=b: st[:, b + i:b + i + 1]
                last_stat = list(hslot_free) + [bank_free[i] for i in range(4)]
                d = comp("vector", lambda E, cS=cS: E.tensor_scalar(
                    cS(10), cS(3), 1.0 / DFF, 1e-6, ALU.mult, ALU.add),
                    waits=last_stat)
                d = comp("scalar", lambda E, cS=cS: E.activation(
                    cS(10), cS(10), AF.Sqrt), waits=[d])
                d = comp("vector", lambda E, cS=cS: E.reciprocal(
                    cS(10), cS(10)), waits=[d])
                d = comp("vector", lambda E, cS=cS: E.tensor_mul(
                    cS(8), cS(10), cS(2)), waits=[d])
                d = comp("vector", lambda E, cS=cS: E.tensor_scalar(
                    cS(8), cS(8), 1e-5, None, ALU.max), waits=[d])
                d = comp("vector", lambda E, cS=cS: E.reciprocal(
                    cS(9), cS(8)), waits=[d])
                d = comp("vector", lambda E, cS=cS: E.tensor_mul(
                    cS(4), cS(10), cS(9)), waits=[d])
                d = comp("vector", lambda E, cS=cS: E.tensor_scalar(
                    cS(4), cS(4), 127.0, None, ALU.mult), waits=[d])
                d = comp("vector", lambda E, cS=cS: E.scalar_tensor_tensor(
                    cS(6), cS(8), 0.5, bc[:, 2:3], ALU.mult, ALU.mult),
                    waits=[d])
                hdeps[t] = d
            w3sl = [None] * 6
            dwb = None
            nb = 0
            for g in range(NT // 4):
                ts4 = [4 * g + j for j in range(4)]
                dq3s = [None] * 4
                for ti, t in enumerate(ts4):
                    b = t * 12
                    dq3 = None
                    for hv in range(2):
                        drl = dma("scalar", "hrl", big3[:],
                                  h_dram[t * 128:(t + 1) * 128,
                                         hv * 4096:(hv + 1) * 4096],
                                  waits=list(hslot_free) + [rd(big3)])
                        for pch in range(2):
                            pc = hv * 2 + pch
                            du = comp("scalar", lambda E, pch=pch, b=b:
                                      E.activation(
                                sqs[:], big3[:, pch * 2048:(pch + 1) * 2048],
                                AF.Copy, bias=MAGIC, scale=st[:, b + 4:b + 5]),
                                waits=[drl, hdeps[t], rd(sqs)])
                            du = comp("vector", lambda E: E.tensor_scalar(
                                sqs[:], sqs[:], MAGIC, 127.0,
                                ALU.subtract, ALU.min), waits=[du])
                            du = comp("vector", lambda E: E.tensor_scalar(
                                aux[:], sqs[:], -128.0, None, ALU.max),
                                waits=[du, rd(aux)])
                            setrd(sqs, du)
                            dq3 = transpose16(
                                aux, du, fm_dst(fm, 512, pc * 16, ti * 128),
                                extra_waits=[rd(fm)])
                            setrd(aux, dq3)
                        setrd(big3, dq3)
                    dq3s[ti] = dq3
                dx0, s0 = load_xtile(ts4[0])
                dx1, s1 = load_xtile(ts4[1])
                dx2 = dma("sync", "xl2", qraw[:],
                          x_dram[ts4[2] * 128:(ts4[2] + 1) * 128, :],
                          waits=[rd(qraw), phase_entry[0]])
                dx3 = dma("sync", "xl3", hstg[:],
                          x_dram[ts4[3] * 128:(ts4[3] + 1) * 128, :],
                          waits=[rd(hstg), phase_entry[0]] + list(hslot_free))
                xts4 = [xt[s0], xt[s1], qraw, hstg]
                dxs4 = [dx0, dx1, dx2, dx3]
                evdep = [[None] * 4 for _ in range(4)]
                dlast_fm = None
                for p in range(4):
                    banks = [(nb + j) % 6 for j in range(4)]
                    nb += 4
                    dlast = None
                    for i in range(64):
                        li = p * 64 + i
                        ssl = li % 6
                        dwd = dma("sync", f"w3a{ssl}",
                                  w3b[:, ssl * 512:(ssl + 1) * 512],
                                  ag2_out[w3t_rows(i):w3t_rows(i) + 128,
                                          p * 512:(p + 1) * 512],
                                  waits=[w3sl[ssl], dag2])
                        for ti in range(4):
                            bk = banks[ti]
                            dlast = comp("tensor", lambda E, i=i, ti=ti, bk=bk,
                                         ssl=ssl: E.matmul(
                                pb[bk][:],
                                fm[:, i * 512 + ti * 128:i * 512 + (ti + 1) * 128],
                                w3b[:, ssl * 512:(ssl + 1) * 512],
                                start=(i == 0), stop=(i == 63)),
                                waits=([dwd] + dq3s + [bank_free[b_]
                                                       for b_ in banks])
                                if (i == 0 and ti == 0) else
                                ([dwd] if ti == 0 else []))
                        w3sl[ssl] = dlast
                    dlast_fm = dlast
                    for ti in range(4):
                        bk = banks[ti]
                        bank_fill[bk] = dlast
                        b = ts4[ti] * 12
                        dv = comp("vector", lambda E, bk=bk, ti=ti, p=p, b=b:
                                  E.scalar_tensor_tensor(
                            xts4[ti][:, p * 512:(p + 1) * 512], pb[bk][:],
                            st[:, b + 6:b + 7],
                            xts4[ti][:, p * 512:(p + 1) * 512],
                            ALU.mult, ALU.add),
                            waits=[bank_fill[bk], dxs4[ti]])
                        bank_free[bk] = dv
                        evdep[ti][p] = dv
                for ti in range(4):
                    t = ts4[ti]
                    dwb = dma("scalar", wsem, dst[t * 128:(t + 1) * 128, :],
                              xts4[ti][:], waits=evdep[ti] + [dwb])
                    if ti == 0:
                        xt_free[s0] = dwb
                    elif ti == 1:
                        xt_free[s1] = dwb
                setrd(qraw, dwb)
                setrd(hstg, dwb)
                setrd(fm, dlast_fm)
            return

        hslot_free_attn = [None] * 8
        ATT = {"gate_w": [None, None], "proj_w": [None, None],
               "qkv_done": None, "q_done": None}

        def attn_phase():
            dq = None
            for t in range(NT):
                dxt, slot = load_xtile(t)
                deps = ln_stats(t, slot, dxt, False)
                b = t * 12
                cS = lambda i, b=b: st[:, b + i:b + i + 1]
                d = comp("vector", lambda E, cS=cS: E.tensor_scalar(
                    cS(9), cS(0), 1.0 / C, None, ALU.mult), waits=deps)
                d = comp("vector", lambda E, cS=cS: E.tensor_scalar(
                    cS(10), cS(1), 1.0 / C, None, ALU.mult), waits=[d])
                d = comp("vector", lambda E, cS=cS: E.scalar_tensor_tensor(
                    cS(11), cS(9), 1.0, cS(9), ALU.mult, ALU.mult), waits=[d])
                d = comp("vector", lambda E, cS=cS: E.tensor_sub(
                    cS(10), cS(10), cS(11)), waits=[d])
                d = comp("vector", lambda E, cS=cS: E.tensor_scalar(
                    cS(10), cS(10), 1.0, 1e-5, ALU.mult, ALU.add), waits=[d])
                d = comp("scalar", lambda E, cS=cS: E.activation(
                    cS(10), cS(10), AF.Sqrt), waits=[d])
                d = comp("vector", lambda E, cS=cS: E.reciprocal(
                    cS(10), cS(10)), waits=[d])
                d = comp("vector", lambda E, cS=cS: E.scalar_tensor_tensor(
                    cS(11), cS(9), -1.0, cS(10), ALU.mult, ALU.mult), waits=[d])
                dxl = comp("scalar", lambda E, slot=slot, cS=cS: E.activation(
                    aux[:], xt[slot][:], AF.Identity, bias=cS(11), scale=cS(10)),
                    waits=[d, rd(aux)])
                dq = transpose16(aux, dxl, fm_dst(fm, 1024, 0, t * 128),
                                 extra_waits=[rd(fm)])
                setrd(aux, dq)
                xt_free[slot] = dxl

            wslot_free = [None, None]
            kin_done = []
            vin_done = []
            fc_list = [("k", i) for i in range(8)] + [("v", i) for i in range(8)] \
                + [("q", i) for i in range(8)]
            dmmlast = None
            dagkv = [None, None]
            last_cast = [rd(big3), rd(big3)]
            for kind, i in fc_list:
                fc = {"q": 0, "k": 8, "v": 16}[kind] + i
                sl = fc % 2
                fcol = {"q": 0, "k": C, "v": 2 * C}[kind] + i * 256
                dL0 = dma("sync", "qw0",
                          big3[:, 0:2048].rearrange("p (n f) -> p n f", n=8),
                          wqkv[0:1024, fcol:fcol + 256]
                          .rearrange("(n p) f -> p n f", p=128),
                          waits=[last_cast[0]])
                dL1 = dma("scalar", "qw1",
                          big3[:, 2048:4096].rearrange("p (n f) -> p n f", n=8),
                          wqkv[1024:2048, fcol:fcol + 256]
                          .rearrange("(n p) f -> p n f", p=128),
                          waits=[last_cast[1]])
                dc0 = comp("scalar", lambda E, sl=sl: E.activation(
                    wbuf[:, sl * 4096:sl * 4096 + 2048], big3[:, 0:2048],
                    AF.Copy),
                    waits=[dL0, wslot_free[sl], rd(wbuf)])
                last_cast[0] = dc0
                dwd = comp("scalar", lambda E, sl=sl: E.activation(
                    wbuf[:, sl * 4096 + 2048:(sl + 1) * 4096],
                    big3[:, 2048:4096], AF.Copy), waits=[dL1])
                last_cast[1] = dwd
                setrd(big3, dwd)
                if kind != "v":
                    lcol = i * 256
                    dcs = dma("sync", "csm0",
                              sqs[:].rearrange("p (t f) -> p t f", f=256),
                              cosb[:, lcol:lcol + 256]
                              .rearrange("(t p) f -> p t f", p=128),
                              waits=[rd(sqs)])
                    dsn = dma("scalar", "csm1",
                              hstg[:].rearrange("p (t f) -> p t f", f=256),
                              sinb[:, lcol:lcol + 256]
                              .rearrange("(t p) f -> p t f", p=128),
                              waits=[rd(hstg)])
                for t in range(NT):
                    bank = t % 3
                    dmm = None
                    for k in range(16):
                        dmm = comp("tensor", lambda E, k=k, t=t, bank=bank,
                                   sl=sl: E.matmul(
                            pb[bank][:, 0:256],
                            fm[:, k * 1024 + t * 128:k * 1024 + (t + 1) * 128],
                            wbuf[:, sl * 4096 + k * 256:sl * 4096 + (k + 1) * 256],
                            start=(k == 0), stop=(k == 15)),
                            waits=[dwd, bank_free[bank], dq] if k == 0 else [])
                    bank_fill[bank] = dmm
                    dmmlast = dmm
                    if kind == "v":
                        dvc = comp("vector", lambda E, bank=bank: E.tensor_copy(
                            smb[:, 0:256], pb[bank][:, 0:256]),
                            waits=[bank_fill[bank], rd(smb)])
                        dst_v = v_in.rearrange("(h tt) dd -> tt h dd", h=H)[
                            t * 128:(t + 1) * 128, i * 2:(i + 1) * 2, :]
                        dvw = dma("scalar", "kvw", dst_v,
                                  smb[:, 0:256]
                                  .rearrange("p (n dd) -> p n dd", n=2),
                                  waits=[dvc])
                        setrd(smb, dvw)
                        vin_done.append(dvw)
                        bank_free[bank] = dvc
                        continue
                    dcp = comp("scalar", lambda E, bank=bank, t=t: E.activation(
                        qraw[:, t * 256:(t + 1) * 256], pb[bank][:, 0:256],
                        AF.Copy), waits=[bank_fill[bank], rd(qraw)])
                    bank_free[bank] = dcp
                    d1 = comp("vector", lambda E, t=t: E.tensor_mul(
                        qst[:, 0:256], qraw[:, t * 256:(t + 1) * 256],
                        sqs[:, t * 256:(t + 1) * 256]),
                        waits=[dcs, dcp, rd(qst)])
                    ybv = qraw[:, t * 256:(t + 1) * 256].rearrange(
                        "p (n two) -> p n two", two=2)
                    d2 = comp("vector", lambda E, ybv=ybv, t=t: E.tensor_mul(
                        qst[:, 256:512].rearrange("p (n two) -> p n two", two=2),
                        ybv[:, :, ::-1],
                        hstg[:, t * 256:(t + 1) * 256]
                        .rearrange("p (n two) -> p n two", two=2)),
                        waits=[dsn, d1])
                    d3 = comp("vector", lambda E: E.tensor_add(
                        qst[:, 0:256], qst[:, 0:256], qst[:, 256:512]),
                        waits=[d2])
                    setrd(qst, d3)
                    setrd(qraw, d2)
                    setrd(sqs, d1)
                    setrd(hstg, d2)
                    for j in range(2):
                        hh = i * 2 + j
                        scol = (qde[:, hh * 4 + (t % 4):hh * 4 + (t % 4) + 1]
                                if kind == "q" else kde[:, hh:hh + 1])
                        d3 = comp("vector", lambda E, j=j, scol=scol:
                                  E.tensor_scalar(
                            aux[:, j * 128:(j + 1) * 128],
                            qst[:, j * 128:(j + 1) * 128], scol, None, ALU.mult),
                            waits=[d3, rd(aux), dep_consts] if j == 0 else [d3])
                    if kind == "q":
                        dtp = transpose16(
                            aux, d3, fm_dst(fm, 1024, 16 + i * 2, t * 128),
                            nt_=2, extra_waits=[rd(fm)])
                        setrd(aux, dtp)
                        ATT["q_done"] = dtp
                    else:
                        dtp = transpose16(
                            aux, d3,
                            lambda j0, n: smb[:, 256 + j0 * 128:
                                              256 + (j0 + n) * 128]
                            .rearrange("p (nn z) -> p nn z", z=128),
                            nt_=2, extra_waits=[rd(smb)])
                        setrd(aux, dtp)
                        dst_k = k_in.rearrange("(h dd) tt -> dd h tt", h=H)[
                            :, i * 2:(i + 1) * 2, t * 128:(t + 1) * 128]
                        dkw = dma("scalar", "kvw", dst_k,
                                  smb[:, 256:512]
                                  .rearrange("p (n tt) -> p n tt", n=2),
                                  waits=[dtp])
                        setrd(smb, dkw)
                        kin_done.append(dkw)
                wslot_free[sl] = dmmlast
                if i == 7 and kind == "k":
                    dagkv[0] = coll(lambda E: E.collective_compute(
                        "AllGather", ALU.bypass,
                        replica_groups=[[0, 1, 2, 3], [4, 5, 6, 7]],
                        ins=[k_in.opt()], outs=[k_out.opt()]),
                        waits=[kin_done[-1]])
                elif i == 7 and kind == "v":
                    dagkv[1] = coll(lambda E: E.collective_compute(
                        "AllGather", ALU.bypass,
                        replica_groups=[[0, 1, 2, 3], [4, 5, 6, 7]],
                        ins=[v_in.opt()], outs=[v_out.opt()]),
                        waits=[vin_done[-1]])
            ATT["qkv_done"] = dmmlast
            ATT["qkv_vdone"] = (vS, sch.n(vS))
            dagk, dagv = dagkv
            dep_kin = kin_done[-1]
            dep_vin = vin_done[-1]
            dzlast_h = [None]

            def emit_stats_gn(pd):
                h, ypb, fill = pd["h"], pd["ypb"], pd["fill"]
                for tt in range(4):
                    i4 = h * 4 + tt
                    comp("vector", lambda E, i4=i4, tt=tt, ypb=ypb:
                         E.reduce_sum(st[:, 96 + i4:97 + i4],
                                      pb[ypb][:, tt * 128:(tt + 1) * 128],
                                      AX), waits=[fill])
                vred = (vS, sch.n(vS))
                dstat = None
                for tt in range(4):
                    i4 = h * 4 + tt
                    dstat = comp("scalar", lambda E, i4=i4, tt=tt, ypb=ypb:
                                 E.activation(
                        sqs[:, 0:128], pb[ypb][:, tt * 128:(tt + 1) * 128],
                        AF.Square, accum_out=st[:, 160 + i4:161 + i4]),
                        waits=[fill, rd(sqs), vred])
                    setrd(sqs, dstat)
                c0 = 96 + 4 * h
                d = comp("vector", lambda E, c0=c0: E.tensor_scalar(
                    st[:, c0 + 128:c0 + 132], st[:, c0:c0 + 4],
                    -1.0 / D, None, ALU.mult), waits=[vred])
                d = comp("vector", lambda E, c0=c0: E.tensor_scalar(
                    st[:, c0 + 64:c0 + 68], st[:, c0 + 64:c0 + 68],
                    1.0 / D, None, ALU.mult), waits=[d, dstat])
                d = comp("vector", lambda E, c0=c0: E.scalar_tensor_tensor(
                    st[:, c0 + 192:c0 + 196], st[:, c0 + 128:c0 + 132], 1.0,
                    st[:, c0 + 128:c0 + 132], ALU.mult, ALU.mult), waits=[d])
                d = comp("vector", lambda E, c0=c0: E.tensor_sub(
                    st[:, c0 + 64:c0 + 68], st[:, c0 + 64:c0 + 68],
                    st[:, c0 + 192:c0 + 196]), waits=[d])
                d = comp("vector", lambda E, c0=c0: E.tensor_scalar(
                    st[:, c0 + 64:c0 + 68], st[:, c0 + 64:c0 + 68],
                    1.0, 1e-5, ALU.mult, ALU.add), waits=[d])
                d = comp("scalar", lambda E, c0=c0: E.activation(
                    st[:, c0 + 192:c0 + 196], st[:, c0 + 64:c0 + 68],
                    AF.Sqrt), waits=[d])
                pd["drs"] = comp("vector", lambda E, c0=c0: E.reciprocal(
                    st[:, c0 + 192:c0 + 196], st[:, c0 + 192:c0 + 196]),
                    waits=[d])

            def emit_ztt(pd, tt):
                h, ypb = pd["h"], pd["ypb"]
                i4 = h * 4 + tt
                gsl = g32sl(tt, h)
                zsl = gch[:, tt * 2048 + h * 128:tt * 2048 + (h + 1) * 128]
                dza = comp("vector", lambda E, tt=tt, ypb=ypb, i4=i4:
                           E.tensor_scalar(
                    qst[:, 0:128], pb[ypb][:, tt * 128:(tt + 1) * 128],
                    st[:, 224 + i4:225 + i4], st[:, 288 + i4:289 + i4],
                    ALU.add, ALU.mult),
                    waits=[pd["drs"], pd["gate"], rd(qst)])
                d1 = comp("vector", lambda E, gsl=gsl: E.tensor_mul(
                    qst[:, 128:256], qst[:, 0:128], gsl),
                    waits=[dza])
                dt1 = comp("tensor", lambda E: E.transpose(
                    pb[5][:, 0:128], qst[:, 128:256], idnf[:]),
                    waits=[d1, bank_free[5], dep_consts])
                dt2 = comp("tensor", lambda E, gsl=gsl: E.transpose(
                    pb[5][:, 128:256], gsl, idnf[:]), waits=[dt1])
                setrd(qst, dt1)
                setrd(g32buf(tt, h), dt2)
                bank_fill[5] = dt2
                ev1 = comp("vector", lambda E, h=h: E.tensor_scalar(
                    sqs[:, 0:128], pb[5][:, 128:256],
                    gnb[:, h:h + 1], None, ALU.mult),
                    waits=[dt2, rd(sqs)])
                ev2 = comp("vector", lambda E, h=h, zsl=zsl:
                           E.scalar_tensor_tensor(
                    zsl, pb[5][:, 0:128], gnw[:, h:h + 1],
                    sqs[:, 0:128], ALU.mult, ALU.add),
                    waits=[ev1, rd(gch)])
                setrd(sqs, ev2)
                bank_free[5] = ev2
                dzlast_h[0] = ev2
                pd["dza"] = dza

            pend = [None]

            def drain_step():
                pd = pend[0]
                if pd is None:
                    return
                if pd["stage"] == 0:
                    emit_stats_gn(pd)
                else:
                    emit_ztt(pd, pd["stage"] - 1)
                pd["stage"] += 1
                if pd["stage"] == 5:
                    bank_free[pd["ypb"]] = pd["dza"]
                    pend[0] = None

            kslot_free = [None, None, None]
            for chunk in range(2):
                dgate = emit_gate(chunk)
                plist = [p for p, pl in enumerate(PASS_LIST) if pl[0] == chunk]
                np_ = len(plist)
                for h in range(H):
                    ypb = 3 + (h % 2)
                    nacc = 0
                    dav = None
                    for pi, p in enumerate(plist):
                        _, src = PASS_LIST[p]
                        ksl = pi % 3
                        if src[0] == "diag":
                            ksrc = k_in[h * 128:(h + 1) * 128,
                                        chunk * 512:(chunk + 1) * 512]
                            vsrc = v_in[h * TOK + chunk * 512:
                                        h * TOK + (chunk + 1) * 512, :]
                            kw = [kslot_free[ksl], dep_kin]
                            vw = [kslot_free[ksl], dep_vin]
                        else:
                            slot, sub = src[1], src[2]
                            ksrc = k_out[slot * 2048 + h * 128:
                                         slot * 2048 + (h + 1) * 128,
                                         sub * 512:(sub + 1) * 512]
                            vsrc = v_out[slot * (H * TOK) + h * TOK + sub * 512:
                                         slot * (H * TOK) + h * TOK
                                         + (sub + 1) * 512, :]
                            kw = [kslot_free[ksl], dagk]
                            vw = [kslot_free[ksl], dagv]
                        dk = dma("sync", f"klm{ksl}", kst[:, ksl * 512:(ksl + 1) * 512],
                                 ksrc, waits=kw)
                        dvv = dma("sync", f"vlm{ksl}",
                                  vst[:, ksl * 512:(ksl + 1) * 512]
                                  .rearrange("p (n dd) -> p n dd", n=4),
                                  vsrc.rearrange("(n p) dd -> p n dd", p=128),
                                  waits=vw)
                        for q in range(4):
                            sbank = q % 3
                            dsc = comp("tensor", lambda E, q=q, ksl=ksl, h=h,
                                       chunk=chunk, sbank=sbank: E.matmul(
                                pb[sbank][:],
                                kst[:, ksl * 512 + q * 128:
                                    ksl * 512 + (q + 1) * 128],
                                fm[:, 16384 + h * 1024 + chunk * 512:
                                   16384 + h * 1024 + (chunk + 1) * 512],
                                start=True, stop=True),
                                waits=[dk, bank_free[sbank], ATT["q_done"]])
                            bank_fill[sbank] = dsc
                            scol = sct[:, (h * NPASS + p) * 4 + q:
                                       (h * NPASS + p) * 4 + q + 1]
                            if src[0] == "diag":
                                dev = comp("vector", lambda E, q=q, sbank=sbank,
                                           scol=scol: E.scalar_tensor_tensor(
                                    smb[:, q * 512:(q + 1) * 512], pb[sbank][:],
                                    scol, tri[:, q * 512:(q + 1) * 512],
                                    ALU.mult, ALU.mult),
                                    waits=[bank_fill[sbank], rd(smb),
                                           dep_consts])
                            else:
                                dev = comp("scalar", lambda E, q=q, sbank=sbank,
                                           scol=scol: E.activation(
                                    smb[:, q * 512:(q + 1) * 512], pb[sbank][:],
                                    AF.Copy, scale=scol),
                                    waits=[bank_fill[sbank], rd(smb)])
                            bank_free[sbank] = dev
                            last_pass = (pi == np_ - 1)
                            for tt in range(4):
                                dav = comp("tensor", lambda E, q=q, tt=tt,
                                           ksl=ksl, ypb=ypb,
                                           first=(nacc == 0 and tt == 0),
                                           stp=(last_pass and q == 3 and tt == 3):
                                           E.matmul(
                                    pb[ypb][:, tt * 128:(tt + 1) * 128],
                                    smb[:, q * 512 + tt * 128:
                                        q * 512 + (tt + 1) * 128],
                                    vst[:, ksl * 512 + q * 128:
                                        ksl * 512 + (q + 1) * 128],
                                    start=first, stop=stp,
                                    skip_group_check=True),
                                    waits=[dev, dvv, bank_free[ypb]]
                                    if tt == 0 else [])
                            setrd(smb, dav)
                            nacc += 1
                        kslot_free[ksl] = dav
                        drain_step()
                    bank_fill[ypb] = dav
                    if chunk == 1 and h == H - 1:
                        setrd(fm, dav)
                    while pend[0] is not None:
                        drain_step()
                    pend[0] = {"h": h, "ypb": ypb, "fill": dav,
                               "gate": dgate, "stage": 0}
                while pend[0] is not None:
                    drain_step()
                attn_tail(chunk, dzlast_h[0])
            for i_ in range(8):
                hslot_free_attn[i_] = rd(hstg)
            return

        def g32buf(tt, h):
            if tt in (0, 1):
                return big3
            if tt == 2:
                return qraw
            return hstg

        def g32sl(tt, h, w=1):
            if tt == 0:
                return big3[:, h * 128:(h + w) * 128]
            if tt == 1:
                return big3[:, 2048 + h * 128:2048 + (h + w) * 128]
            if tt == 2:
                return qraw[:, h * 128:(h + w) * 128]
            return hstg[:, h * 128:(h + w) * 128]

        def emit_gate(chunk):
            wslot_free = ATT["gate_w"]
            dsg = None
            for gc in range(8):
                sl = gc % 2
                dwd = dma("gpsimd", f"gwl{sl}",
                          wbuf[:, sl * 4096:(sl + 1) * 4096]
                          .rearrange("p (n f) -> p n f", n=16),
                          wgd[:, gc * 256:(gc + 1) * 256]
                          .rearrange("(n p) f -> p n f", p=128),
                          waits=[wslot_free[sl], rd(wbuf), ATT["qkv_done"],
                                 rd(gch)])
                for tt in range(4):
                    t = chunk * 4 + tt
                    bank = tt % 3
                    dmm = None
                    for k in range(16):
                        dmm = comp("tensor", lambda E, k=k, t=t, bank=bank,
                                   sl=sl: E.matmul(
                            pb[bank][:, 0:256],
                            fm[:, k * 1024 + t * 128:
                               k * 1024 + (t + 1) * 128],
                            wbuf[:, sl * 4096 + k * 256:
                                 sl * 4096 + (k + 1) * 256],
                            start=(k == 0), stop=(k == 15)),
                            waits=[dwd, bank_free[bank]] if k == 0 else [])
                    bank_fill[bank] = dmm
                    dsg = comp("scalar", lambda E, bank=bank, tt=tt, gc=gc:
                               E.activation(
                        g32sl(tt, gc * 2, 2),
                        pb[bank][:, 0:256], AF.Silu),
                        waits=[bank_fill[bank], rd(big3), rd(qraw),
                               rd(hstg), ATT.get("qkv_vdone")]
                        + list(hslot_free_attn)
                        if gc == 0 and tt == 0 else
                        [bank_fill[bank]])
                    bank_free[bank] = dsg
                    wslot_free[sl] = dmm
                    setrd(wbuf, dmm)
            return dsg

        def attn_tail(chunk, dzlast):
            wslot2 = ATT["proj_w"]
            dwb = None
            for half in range(2):
                dxts = [None, None]
                dadds = [None, None]
                for pc in range(8):
                    sl = pc % 2
                    dwd = dma("gpsimd", f"gwl{sl}",
                              wbuf[:, sl * 4096:(sl + 1) * 4096]
                              .rearrange("p (n f) -> p n f", n=16),
                              wpd[:, pc * 256:(pc + 1) * 256]
                              .rearrange("(n p) f -> p n f", p=128),
                              waits=[wslot2[sl], rd(wbuf)])
                    for tt2 in range(2):
                        tt = half * 2 + tt2
                        t = chunk * 4 + tt
                        if pc == 0:
                            dxts[tt2] = load_xtile(t)
                        bank = (pc * 2 + tt2) % 3
                        dmm = None
                        for hh in range(16):
                            dmm = comp("tensor", lambda E, hh=hh, tt=tt,
                                       bank=bank, sl=sl: E.matmul(
                                pb[bank][:, 0:256],
                                gch[:, tt * 2048 + hh * 128:
                                    tt * 2048 + (hh + 1) * 128],
                                wbuf[:, sl * 4096 + hh * 256:
                                     sl * 4096 + (hh + 1) * 256],
                                start=(hh == 0), stop=(hh == 15)),
                                waits=[dwd, bank_free[bank], dzlast]
                                if hh == 0 else [])
                        bank_fill[bank] = dmm
                        setrd(gch, dmm)
                        setrd(wbuf, dmm)
                        slot = dxts[tt2][1]
                        dadds[tt2] = comp("vector",
                                          lambda E, bank=bank, pc=pc, slot=slot:
                                          E.scalar_tensor_tensor(
                            xt[slot][:, pc * 256:(pc + 1) * 256],
                            pb[bank][:, 0:256], 1.0,
                            xt[slot][:, pc * 256:(pc + 1) * 256],
                            ALU.mult, ALU.add),
                            waits=[bank_fill[bank], dxts[tt2][0], dadds[tt2]])
                        bank_free[bank] = dadds[tt2]
                        wslot2[sl] = dmm
                for tt2 in range(2):
                    tt = half * 2 + tt2
                    t = chunk * 4 + tt
                    slot = dxts[tt2][1]
                    dwb = dma("scalar", "xw2", x_dram[t * 128:(t + 1) * 128, :],
                              xt[slot][:], waits=[dadds[tt2], dwb])
                    xt_free[slot] = dwb
            return

        if DO_FFN1:
            ffn_phase(x_dram, "xw1")
            phase_entry[0] = (SM["xw1"], sch.n(SM["xw1"]))
        if DO_ATTN:
            attn_phase()
            phase_entry[0] = (SM["xw2"], sch.n(SM["xw2"]))
        if DO_FFN2:
            ffn_phase(out_c, "xw3")
        else:
            dma("gpsimd", "xw3", out_c, x_dram,
                waits=[phase_entry[0]])
        if DBG:
            dma("sync", "cst", dbg_st[:, 0:96], st[:, 0:96],
                waits=[(vS, sch.n(vS)), (sS, sch.n(sS)), (tS, sch.n(tS))])
            dma("sync", "cst", dbg_row[0:1, 128:135], srow[0:1, 128:135])
            dma("sync", "cst", dbg_bc[:, 0:6], bc[:, 0:6])
            dma("sync", "cst", dbg_afm, fm[:, 0:16384])
            dma("gpsimd", "gdbg", dbg_h, h_dram[0:128, :])
        final_waits = [(s, sch.n(s)) for s in SM.values()]
        final_waits += [(vS, sch.n(vS)), (sS, sch.n(sS)), (tS, sch.n(tS)),
                        (gS, sch.n(gS)), (ccS, sch.n(ccS))]
        sch.add("sync", lambda E: E.nop(), waits=final_waits)

        @block.sync
        def _(E):
            sch.replay("sync", E)

        @block.gpsimd
        def _(E):
            sch.replay("gpsimd", E)

        @block.vector
        def _(E):
            sch.replay("vector", E)

        @block.scalar
        def _(E):
            sch.replay("scalar", E)

        @block.tensor
        def _(E):
            sch.replay("tensor", E)

    return nc


def kernel(L0_attn_x, L0_attn_w_qkv, L0_attn_w_g, L0_attn_w_proj,
           L0_attn_gn_weight, L0_attn_gn_bias, L0_ffn_w1, L0_ffn_w2, L0_ffn_w3):
    x = np.asarray(L0_attn_x, dtype=np.float32)
    shared = _shared_inputs(np.asarray(L0_attn_w_qkv, np.float32),
                            np.asarray(L0_attn_w_g, np.float32),
                            np.asarray(L0_attn_w_proj, np.float32),
                            np.asarray(L0_attn_gn_weight, np.float32),
                            np.asarray(L0_attn_gn_bias, np.float32))
    in_maps = [
        _host_inputs(core, x,
                     np.asarray(L0_ffn_w1, np.float32),
                     np.asarray(L0_ffn_w2, np.float32),
                     np.asarray(L0_ffn_w3, np.float32), shared)
        for core in range(NCORES)
    ]
    nc = build_nc()
    res = run_bass_kernel_spmd(nc, in_maps, core_ids=list(range(NCORES)))
    out = np.zeros((B, T, C), np.float32)
    for core in range(NCORES):
        b = core // 4
        cA, cB = _chunks_of(core)
        oc = np.asarray(res.results[core]["out_c"])
        out[b, cA * 512:(cA + 1) * 512] = oc[0:512]
        out[b, cB * 512:(cB + 1) * 512] = oc[512:1024]
    return out
